# version 02
# baseline (speedup 1.0000x reference)
import numpy as np
from contextlib import ExitStack

import concourse.bass as bass
import concourse.mybir as mybir
from concourse.bass import IndirectOffsetOnAxis
from concourse.bass_utils import run_bass_kernel_spmd

F32 = mybir.dt.float32
BF16 = mybir.dt.bfloat16
I32 = mybir.dt.int32
AF = mybir.ActivationFunctionType
ALU = mybir.AluOpType
AX = mybir.AxisListType

NCORES = 8
S = 8192
SL = 1024
E = 4096
EB = 32
D = 128
H = 32
KVH = 8
ISL = 512
OT = 8
RCH = 32
SCL = 0.08838834764831845
LN10K = 9.210340371976184
INV2PI = 0.15915494309189535
TWOPI = 6.283185307179586
PIH = 1.5707963267948966
MAGIC = 12582912.0
EPS = 1e-5


def _sched(j):
    out = []
    if j == 0:
        for i in range(8, 12):
            out.append((i, ('L', i - 8)))
        for i in range(12, 56):
            out.append((i, None))
        for i in range(56, 60):
            out.append((i, ('R', i - 56)))
        for i in range(60, 64):
            out.append((i, 'N'))
    else:
        for i in range(8, 12):
            out.append((i, 'W'))
        for i in range(12, 16):
            out.append((i, ('L', i - 12)))
        for i in range(16, 60):
            out.append((i, None))
        for i in range(60, 64):
            out.append((i, ('R', i - 60)))
    return out


SCHED = [_sched(0), _sched(1)]


def build_nc(phases=99.0, dump=None):
    nc = bass.Bass("TRN2", target_bir_lowering=False, debug=False)

    dI = lambda n, s, dt: nc.dram_tensor(n, s, dt, kind="ExternalInput").ap()
    xT = dI("xT", [E, SL], F32)
    wqT = dI("wqT", [E, E], F32)
    wkT = dI("wkT", [E, KVH * D], F32)
    wvT = dI("wvT", [E, KVH * D], F32)
    woT = dI("woT", [E, E], F32)
    utT = dI("utT", [E, E], F32)
    vhsl = dI("vhsl", [E, ISL], F32)
    stP = dI("stP", [128, RCH], F32)
    biasP = dI("biasP", [128, EB], F32)
    utopT = dI("utopT", [64, E], F32)
    vhtop = dI("vhtop", [64, ISL], F32)
    utrT = dI("utrT", [4, E], F32)
    vhtr = dI("vhtr", [4, ISL], F32)
    r4 = dI("r4", [4, 4], F32)
    stop_r = dI("stop_r", [1, 64], F32)
    alpha_r = dI("alpha_r", [1, 64], F32)
    beta_r = dI("beta_r", [1, 64], F32)
    str_r = dI("str_r", [1, 4], F32)
    dv_r = dI("dv_r", [1, 4], F32)
    pos0 = dI("pos0", [1, 1], F32)
    kvidx = dI("kvidx", [128, 64], I32)
    eflags = dI("eflags", [128, 2], F32)

    outT = nc.dram_tensor("outT", [E, SL], F32, kind="ExternalOutput").ap()

    w_loc = nc.dram_tensor("w_loc", [ISL, E], BF16, kind="Internal")
    w_full = nc.dram_tensor("w_full", [E, E], BF16, kind="Internal")
    k_loc = nc.dram_tensor("k_loc", [KVH, D * SL], BF16, kind="Internal")
    k_full = nc.dram_tensor("k_full", [NCORES * KVH, D * SL], BF16, kind="Internal")
    v_loc = nc.dram_tensor("v_loc", [KVH, D * SL], BF16, kind="Internal")
    v_full = nc.dram_tensor("v_full", [NCORES * KVH, D * SL], BF16, kind="Internal")
    att_d = nc.dram_tensor("att_d", [H, 128, SL], BF16, kind="Internal").ap()
    x2_d = nc.dram_tensor("x2_d", [E, SL], F32, kind="Internal").ap()
    x3_d = nc.dram_tensor("x3_d", [E, SL], F32, kind="Internal").ap()

    with ExitStack() as ctx:
        sb = lambda n, s, dt: ctx.enter_context(nc.sbuf_tensor(n, s, dt))
        bufB = sb("bufB", [128, EB * SL], BF16)
        wsb = sb("wsb", [128, 4096], BF16)
        xin = sb("xin", [128, SL], F32)
        xout = sb("xout", [128, SL], F32)
        tsc = sb("tsc", [128, SL], F32)
        sqs = sb("sqs", [128, SL], BF16)
        rows = sb("rows", [128, SL], F32)
        bcs = sb("bcs", [128, 512], F32)
        cos_bf = sb("cos_bf", [128, SL], BF16)
        sin_bf = sb("sin_bf", [128, SL], BF16)
        masks = sb("masks", [128, 10 * 512], BF16)
        ast = sb("ast", [128, 2 * SL], BF16)
        ones_c = sb("ones_c", [128, 1], BF16)
        ones_rf = sb("ones_rf", [1, 128], F32)
        onesF = sb("onesF", [128, 128], F32)
        identf = sb("identf", [128, 128], F32)
        idx_s = sb("idx_s", [128, 64], I32)
        efl_s = sb("efl_s", [128, 2], F32)
        pos_s = sb("pos_s", [1, 1], F32)
        freqc = sb("freqc", [128, 1], F32)
        biasP_s = sb("biasP_s", [128, EB], F32)
        hbP = sb("hbP", [128, EB], F32)
        stP_s = sb("stP_s", [128, RCH], F32)
        vhtop_sb = sb("vhtop_sb", [64, ISL], BF16)
        vhtr_sb = sb("vhtr_sb", [4, ISL], BF16)
        r4_sb = sb("r4_sb", [4, 4], BF16)
        srow = sb("srow", [1, 64], F32)
        arow = sb("arow", [1, 64], F32)
        brow = sb("brow", [1, 64], F32)
        sh_r = sb("sh_r", [1, 64], F32)
        shc = sb("shc", [64, 1], F32)
        str_s = sb("str_s", [1, 4], F32)
        dv_s = sb("dv_s", [1, 4], F32)
        sd4r = sb("sd4r", [1, 4], F32)
        sd4c = sb("sd4c", [4, 1], F32)
        bnd = sb("bnd", [1, 8], F32)
        bcol = sb("bcol", [128, 1], F32)
        rotm = sb("rotm", [128, 128], BF16)
        qtmp = sb("qtmp", [128, SL], BF16)
        c64 = sb("c64", [128, 1], F32)
        g_sb = sb("g_sb", [4, E], BF16)
        frm = ExitStack()
        vh_sb = frm.enter_context(nc.sbuf_tensor("vh_sb", [128, RCH * ISL], BF16))
        ust = frm.enter_context(nc.sbuf_tensor("ust", [128, 2 * 512], BF16))
        wev = frm.enter_context(nc.sbuf_tensor("wev", [128, 2 * 512], BF16))
        utop_sb = frm.enter_context(nc.sbuf_tensor("utop_sb", [64, E], BF16))
        utr_sb = frm.enter_context(nc.sbuf_tensor("utr_sb", [4, E], BF16))
        tmp1 = frm.enter_context(nc.sbuf_tensor("tmp1", [128, SL], F32))
        trow = frm.enter_context(nc.sbuf_tensor("trow", [1, SL], F32))
        mtmp = frm.enter_context(nc.sbuf_tensor("mtmp", [128, 512], BF16))
        frm.close()
        att = ExitStack()
        ksb = att.enter_context(nc.sbuf_tensor("ksb", [128, S], BF16))
        vsb = att.enter_context(nc.sbuf_tensor("vsb", [128, S], BF16))
        qsb = att.enter_context(nc.sbuf_tensor("qsb", [128, 2 * SL], BF16))
        ptile = att.enter_context(nc.sbuf_tensor("ptile", [128, 4 * 512], BF16))
        att.close()
        bufA = sb("bufA", [128, EB * SL], BF16)
        kstage = qsb[:, 0:SL]
        vtr_sb = qsb[:, SL:2 * SL]

        p01 = ctx.enter_context(nc.psum_tensor("p01", [128, 1024], F32))
        p23 = ctx.enter_context(nc.psum_tensor("p23", [128, 1024], F32))
        p45 = ctx.enter_context(nc.psum_tensor("p45", [128, 1024], F32))
        p67 = ctx.enter_context(nc.psum_tensor("p67", [128, 1024], F32))

        sems = {}
        for name in ["T", "V", "C", "G", "DG", "DY", "DC", "CC1", "CC2"]:
            sems[name] = ctx.enter_context(nc.semaphore("s" + name))

        block = ctx.enter_context(nc.Block())

        def emit(which, eng):
            cnt = {k: 0 for k in sems}
            TE = which == "t"
            VE = which == "v"
            SC = which == "c"
            GP = which == "g"
            SY = which == "y"
            t = eng if TE else None
            v = eng if VE else None
            c = eng if SC else None
            g = eng if GP else None
            y = eng if SY else None

            def I(name, ins, k=1):
                cnt[name] += k
                if ins is not None:
                    ins.then_inc(sems[name], k)

            def ICC(name, ins):
                cnt[name] += 1
                if ins is not None:
                    ins.then_inc(sems[name])

            def W(active, name, val=None):
                if active:
                    v_ = cnt[name] if val is None else val
                    if v_ > 0:
                        eng.wait_ge(sems[name], v_)

            def TMILE(ins=None):
                cnt["T"] += 1
                if TE:
                    if ins is not None:
                        ins.then_inc(sems["T"], 1)
                    else:
                        eng.nop().then_inc(sems["T"], 1)
                return cnt["T"]

            def BARRIER():
                for n in sems:
                    W(True, n)

            M = lambda name: cnt[name]

            for (dst, src) in [
                (pos_s[:, :], pos0), (idx_s[:, :], kvidx),
                (biasP_s[:, :], biasP), (stP_s[:, :], stP),
                (utop_sb[:, :], utopT), (vhtop_sb[:, :], vhtop),
                (utr_sb[:, :], utrT), (vhtr_sb[:, :], vhtr),
                (r4_sb[:, :], r4), (efl_s[:, :], eflags),
            ]:
                I("DG", g.dma_start(out=dst, in_=src[:, :]) if GP else None, 16)
            for (dst, src) in [(srow, stop_r), (arow, alpha_r), (brow, beta_r),
                               (str_s, str_r), (dv_s, dv_r)]:
                I("DG", g.dma_start(out=dst[:, :], in_=src[:, :]) if GP else None, 16)
            m_smalls = M("DG")
            I("G", g.memset(ones_c[:, :], 1.0) if GP else None)
            I("G", g.memset(ones_rf[:, :], 1.0) if GP else None)
            I("G", g.memset(onesF[:, :], 1.0) if GP else None)
            I("G", g.memset(mtmp[:, :], 1.0) if GP else None)
            I("G", g.memset(tsc[:, 0:128], 1.0) if GP else None)
            W(GP, "G")
            I("G", g.affine_select(identf[:, :], tsc[:, 0:128], [[1, 128]],
                                   ALU.is_equal, 0.0, channel_multiplier=-1) if GP else None)
            for m in range(4):
                I("G", g.affine_select(masks[:, m * 512:(m + 1) * 512], mtmp[:, :],
                                       [[-1, 512]], ALU.is_ge, 0.0, base=128 * m,
                                       channel_multiplier=1) if GP else None)
            for m in range(4):
                I("G", g.affine_select(masks[:, (4 + m) * 512:(5 + m) * 512], mtmp[:, :],
                                       [[1, 512]], ALU.is_ge, 0.0, base=-128 * m,
                                       channel_multiplier=-1) if GP else None)
            I("G", g.memset(masks[:, 8 * 512:10 * 512], 0.0) if GP else None)
            I("G", g.iota(trow[:, :], [[1, SL]], base=0, channel_multiplier=0,
                          allow_small_or_imprecise_dtypes=True) if GP else None)
            I("G", g.iota(freqc[:, :], [[1, 1]], base=0, channel_multiplier=1,
                          allow_small_or_imprecise_dtypes=True) if GP else None)
            I("G", g.memset(c64[:, :], 64.0) if GP else None)
            W(GP, "G")
            I("G", g.affine_select(bcol[:, :], c64[:, :], [[1, 1]], ALU.is_ge, 0.0,
                                   base=-64, channel_multiplier=1) if GP else None)
            I("G", g.affine_select(sqs[:, 0:128], mtmp[:, 0:128], [[-1, 128]],
                                   ALU.is_equal, 0.0, base=64,
                                   channel_multiplier=1) if GP else None)
            I("G", g.affine_select(sqs[:, 128:256], mtmp[:, 0:128], [[-1, 128]],
                                   ALU.is_equal, 0.0, base=-64,
                                   channel_multiplier=1) if GP else None)
            m_ginit = M("G")
            W(VE, "G", m_ginit)
            I("V", v.tensor_sub(rotm[:, :], sqs[:, 0:128], sqs[:, 128:256]) if VE else None)
            if phases <= 0.05:
                return

            if dump == "gen":
                W(GP, "G", m_ginit)
                I("DG", g.dma_start(out=outT[0:1, 0:1024], in_=trow[:, :]) if GP else None, 16)
                I("DG", g.dma_start(out=outT[128:256, 0:1], in_=freqc[:, :], allow_slow_non_contiguous=True) if GP else None, 16)
                I("DG", g.dma_start(out=outT[256:384, 0:1], in_=bcol[:, :], allow_slow_non_contiguous=True) if GP else None, 16)
                I("DG", g.dma_start(out=outT[384:512, 0:512], in_=masks[:, 0:512]) if GP else None, 16)
                I("DG", g.dma_start(out=outT[512:640, 0:128], in_=identf[:, :]) if GP else None, 16)
                I("DG", g.dma_start(out=outT[640:768, 0:256], in_=sqs[:, 0:256]) if GP else None, 16)
                W(SY, "DG")
                W(SY, "DY")
                return

            W(VE, "G", m_ginit)
            W(VE, "DG", m_smalls)
            I("V", v.tensor_scalar_add(trow[:, :], trow[:, :], pos_s[0:1, 0:1]) if VE else None)
            m_trow = M("V")
            if phases <= 0.055:
                return
            W(VE, "G", m_ginit)
            I("V", v.tensor_sub(freqc[:, :], freqc[:, :], bcol[:, :]) if VE else None)
            m_fq1 = M("V")
            W(SC, "V", m_fq1)
            I("C", c.activation(freqc[:, :], freqc[:, :], AF.Exp,
                                scale=-LN10K / 64.0) if SC else None)
            m_freq0 = M("C")
            W(VE, "C", m_freq0)
            m_freq = M("V")
            if phases <= 0.06:
                return
            W(TE, "V", m_trow)
            W(TE, "G", m_ginit)
            t and t.matmul(p01[:, 0:512], ones_rf[:, :], trow[0:1, 0:512], start=True, stop=True)
            _li = TE and t.matmul(p01[:, 512:1024], ones_rf[:, :], trow[0:1, 512:1024],
                           start=True, stop=True)
            m_tbc = TMILE(_li if TE else None)
            if phases <= 0.08:
                return
            W(VE, "T", m_tbc)
            W(VE, "V", m_freq)
            I("V", v.tensor_copy(tmp1[:, 0:512], p01[:, 0:512]) if VE else None)
            I("V", v.tensor_copy(tmp1[:, 512:1024], p01[:, 512:1024]) if VE else None)
            if phases <= 0.09:
                return
            I("V", v.tensor_scalar_mul(xin[:, :], tmp1[:, :], freqc[:, :]) if VE else None)
            if phases <= 0.1:
                return

            def sin_table(dst_bf, _unused=None):
                I("V", v.tensor_scalar_mul(xout[:, :], xin[:, :], INV2PI) if VE else None)
                I("V", v.tensor_scalar_add(tmp1[:, :], xout[:, :], MAGIC) if VE else None)
                I("V", v.tensor_scalar_add(tmp1[:, :], tmp1[:, :], -MAGIC) if VE else None)
                I("V", v.tensor_sub(xout[:, :], xout[:, :], tmp1[:, :]) if VE else None)
                I("V", v.tensor_scalar_mul(xout[:, :], xout[:, :], TWOPI) if VE else None)
                mv = M("V")
                W(SC, "V", mv)
                I("C", c.activation(tmp1[:, :], xout[:, :], AF.Sin) if SC else None)
                mc = M("C")
                W(VE, "C", mc)
                I("V", v.tensor_copy(dst_bf[:, :], tmp1[:, :]) if VE else None)
                return M("V")

            sin_table(sin_bf, False)
            I("V", v.tensor_scalar_add(xin[:, :], xin[:, :], PIH) if VE else None)
            m_tabs = sin_table(cos_bf)
            if phases <= 0.12:
                return

            I("V", v.tensor_mul(sh_r[:, :], srow[:, :], arow[:, :]) if VE else None)
            I("V", v.tensor_add(sh_r[:, :], sh_r[:, :], brow[:, :]) if VE else None)
            I("V", v.tensor_relu(sh_r[:, :], sh_r[:, :]) if VE else None)
            I("V", v.tensor_relu(dv_s[:, :], dv_s[:, :]) if VE else None)
            I("V", v.tensor_mul(sd4r[:, :], str_s[:, :], dv_s[:, :]) if VE else None)
            I("V", v.tensor_scalar_mul(hbP[:, :], biasP_s[:, :], 0.5) if VE else None)
            for sl8 in range(4):
                I("V", v.tensor_scalar_max(masks[:, sl8 * 512:(sl8 + 1) * 512],
                                           masks[:, sl8 * 512:(sl8 + 1) * 512],
                                           efl_s[:, 0:1]) if VE else None)
            for sl8 in range(4):
                I("V", v.tensor_scalar_max(masks[:, (4 + sl8) * 512:(5 + sl8) * 512],
                                           masks[:, (4 + sl8) * 512:(5 + sl8) * 512],
                                           efl_s[:, 1:2]) if VE else None)
            I("V", v.tensor_scalar_max(masks[:, 8 * 512:9 * 512],
                                       masks[:, 8 * 512:9 * 512], efl_s[:, 0:1]) if VE else None)
            I("V", v.tensor_scalar_max(masks[:, 9 * 512:10 * 512],
                                       masks[:, 9 * 512:10 * 512], efl_s[:, 1:2]) if VE else None)
            m_smallprep = M("V")
            if phases <= 0.15:
                return
            W(TE, "V", m_smallprep)
            t and t.matmul(p45[0:64, 0:1], sh_r[:, :], ones_rf[0:1, 0:1], start=True, stop=True)
            _li = TE and t.matmul(p45[0:4, 1:2], sd4r[:, :], ones_rf[0:1, 0:1], start=True, stop=True)
            m_cols = TMILE(_li if TE else None)
            W(VE, "T", m_cols)
            I("V", v.tensor_copy(shc[:, :], p45[0:64, 0:1]) if VE else None)
            I("V", v.tensor_copy(sd4c[:, :], p45[0:4, 1:2]) if VE else None)
            I("V", v.tensor_scalar(utop_sb[:, :], utop_sb[:, :], shc[:, :], None,
                                   ALU.mult) if VE else None)
            I("V", v.tensor_scalar(utr_sb[:, :], utr_sb[:, :], sd4c[:, :], None,
                                   ALU.mult) if VE else None)
            m_usc = M("V")
            if phases <= 0.18:
                return
            for otl in range(8):
                if otl == 0:
                    W(TE, "V", m_usc)
                else:
                    W(TE, "V")
                _li = TE and t.matmul(p45[0:4, 0:512], r4_sb[:, :],
                               utr_sb[:, otl * 512:(otl + 1) * 512], start=True, stop=True)
                mt = TMILE(_li if TE else None)
                W(VE, "T", mt)
                I("V", v.tensor_copy(g_sb[:, otl * 512:(otl + 1) * 512],
                                     p45[0:4, 0:512]) if VE else None)
            m_g = M("V")
            if phases <= 0.2:
                return

            if dump == "tab0":
                W(GP, "V", m_tabs)
                I("DG", g.dma_start(out=outT[0:128, :], in_=cos_bf[:, :]) if GP else None, 16)
                I("DG", g.dma_start(out=outT[128:256, :], in_=sin_bf[:, :]) if GP else None, 16)
                I("DG", g.dma_start(out=outT[256:384, 0:1], in_=freqc[:, :], allow_slow_non_contiguous=True) if GP else None, 16)
                I("DG", g.dma_start(out=outT[384:512, 0:128], in_=rotm[:, :]) if GP else None, 16)
                W(SY, "DG")
                W(SY, "DY")
                return

            vh_view = vhsl.rearrange("(rc p) i -> p rc i", p=128)
            I("DG", g.dma_start(out=vh_sb[:, :].rearrange("p (rc i) -> p rc i", rc=RCH),
                                in_=vh_view) if GP else None, 16)
            m_vh = M("DG")
            W(VE, "DG", m_vh)
            for rc in range(RCH):
                I("V", v.tensor_scalar(vh_sb[:, rc * 512:(rc + 1) * 512],
                                       vh_sb[:, rc * 512:(rc + 1) * 512],
                                       stP_s[:, rc:rc + 1], None, ALU.mult) if VE else None)
            m_vhs = M("V")

            frm_marks = {}
            wev_marks = [0, 0]
            prev_evac = [0]
            for ot in range(OT):
                for rc in range(RCH):
                    par = (ot * RCH + rc) % 2
                    prev = ot * RCH + rc - 2
                    if prev >= 0:
                        W(GP, "T", frm_marks[prev])
                    src = utT[rc * 128:(rc + 1) * 128, ot * 512:(ot + 1) * 512]
                    I("DG", g.dma_start(out=ust[:, par * 512:(par + 1) * 512],
                                        in_=src) if GP else None, 16)
                    m_u = M("DG")
                    if ot == 0 and rc == 0:
                        W(TE, "V", m_vhs)
                        W(TE, "V", m_g)
                    if rc == 0 and ot > 0:
                        W(TE, "V", prev_evac[0])
                    W(TE, "DG", m_u)
                    _li = None
                    for ic in range(4):
                        acc = (p01 if ic < 2 else p23)[:, (ic % 2) * 512:(ic % 2 + 1) * 512]
                        _li = TE and t.matmul(acc,
                                              vh_sb[:, rc * 512 + ic * 128:rc * 512 + (ic + 1) * 128],
                                              ust[:, par * 512:(par + 1) * 512],
                                              start=(rc == 0), stop=False)
                    frm_marks[ot * RCH + rc] = TMILE(_li if TE else None)
                for ic in range(4):
                    wpar = ic % 2
                    acc = (p01 if ic < 2 else p23)[:, (ic % 2) * 512:(ic % 2 + 1) * 512]
                    t and t.matmul(acc, vhtop_sb[:, ic * 128:(ic + 1) * 128],
                                   utop_sb[:, ot * 512:(ot + 1) * 512], start=False, stop=False)
                    W(TE, "DY", wev_marks[wpar])
                    _li = TE and t.matmul(acc, vhtr_sb[:, ic * 128:(ic + 1) * 128],
                                   g_sb[:, ot * 512:(ot + 1) * 512], start=False, stop=True)
                    mt = TMILE(_li if TE else None)
                    W(VE, "T", mt)
                    W(VE, "DY", wev_marks[wpar])
                    I("V", v.tensor_copy(wev[:, wpar * 512:(wpar + 1) * 512], acc) if VE else None)
                    mv = M("V")
                    W(SY, "V", mv)
                    I("DY", y.dma_start(out=w_loc.ap()[ic * 128:(ic + 1) * 128,
                                                       ot * 512:(ot + 1) * 512],
                                        in_=wev[:, wpar * 512:(wpar + 1) * 512]) if SY else None, 16)
                    wev_marks[wpar] = M("DY")
                prev_evac[0] = M("V")
            m_wloc = M("DY")
            if phases <= 0.5:
                return
            W(GP, "DY", m_wloc)
            ICC("CC1", g.collective_compute(
                "AllGather", ALU.bypass, replica_groups=[list(range(8))],
                ins=[w_loc.ap()], outs=[w_full.ap()]) if GP else None)
            if phases <= 0.7:
                return

            if dump == "tab":
                W(GP, "V", m_tabs)
                I("DG", g.dma_start(out=outT[0:128, :], in_=cos_bf[:, :]) if GP else None, 16)
                I("DG", g.dma_start(out=outT[128:256, :], in_=sin_bf[:, :]) if GP else None, 16)
                I("DG", g.dma_start(out=outT[256:384, 0:1], in_=freqc[:, :], allow_slow_non_contiguous=True) if GP else None, 16)
                I("DG", g.dma_start(out=outT[384:512, 0:128], in_=rotm[:, :]) if GP else None, 16)
                W(SY, "DG")
                W(SY, "DY")
                return

            castm = []
            stat_marks = []
            for icx in range(EB):
                W(SY, "V", m_tabs)
                W(SY, "DY", m_wloc)
                if icx > 0:
                    W(SY, "V", castm[icx - 1])
                I("DY", y.dma_start(out=xin[:, :],
                                    in_=xT[icx * 128:(icx + 1) * 128, :]) if SY else None, 16)
                m_x = M("DY")
                W(VE, "DY", m_x)
                if icx == 0:
                    W(VE, "DY", m_wloc)
                I("V", v.tensor_copy(bufA[:, icx * SL:(icx + 1) * SL], xin[:, :]) if VE else None)
                castm.append(M("V"))
                if icx >= 1:
                    W(SC, "T", stat_marks[icx - 1])
                W(SC, "V", castm[icx])
                I("C", c.activation(sqs[:, :], bufA[:, icx * SL:(icx + 1) * SL],
                                    AF.Square) if SC else None)
                m_sq = M("C")
                W(TE, "C", m_sq)
                first, last = icx == 0, icx == EB - 1
                t and t.matmul(p67[0:1, 0:512], ones_c[:, :],
                               bufA[:, icx * SL:icx * SL + 512], start=first, stop=last)
                t and t.matmul(p67[64:65, 0:512], ones_c[:, :],
                               bufA[:, icx * SL + 512:(icx + 1) * SL], start=first, stop=last)
                t and t.matmul(p67[0:1, 512:1024], ones_c[:, :],
                               sqs[:, 0:512], start=first, stop=last)
                _li = TE and t.matmul(p67[64:65, 512:1024], ones_c[:, :],
                               sqs[:, 512:1024], start=first, stop=last)
                stat_marks.append(TMILE(_li if TE else None))
            if phases <= 0.9:
                return

            def ln_finalize(y_buf):
                W(VE, "T", M("T"))
                for r in (0, 64):
                    rr = slice(r, r + 1)
                    I("V", v.tensor_scalar_mul(rows[rr, 0:512], p67[rr, 0:512],
                                               1.0 / E) if VE else None)
                    I("V", v.tensor_scalar_mul(rows[rr, 512:1024], p67[rr, 512:1024],
                                               1.0 / E) if VE else None)
                    I("V", v.tensor_mul(bcs[rr, 0:512], rows[rr, 0:512],
                                        rows[rr, 0:512]) if VE else None)
                    I("V", v.tensor_sub(rows[rr, 512:1024], rows[rr, 512:1024],
                                        bcs[rr, 0:512]) if VE else None)
                    I("V", v.tensor_scalar_add(rows[rr, 512:1024], rows[rr, 512:1024],
                                               EPS) if VE else None)
                m_var = M("V")
                W(SC, "V", m_var)
                for r in (0, 64):
                    rr = slice(r, r + 1)
                    I("C", c.activation(rows[rr, 512:1024], rows[rr, 512:1024],
                                        AF.Ln) if SC else None)
                    I("C", c.activation(rows[rr, 512:1024], rows[rr, 512:1024], AF.Exp,
                                        scale=-0.5) if SC else None)
                m_rstd = M("C")
                W(VE, "C", m_rstd)
                for r in (0, 64):
                    rr = slice(r, r + 1)
                    I("V", v.tensor_mul(rows[rr, 0:512], rows[rr, 0:512],
                                        rows[rr, 512:1024]) if VE else None)
                    I("V", v.tensor_scalar_mul(rows[rr, 0:512], rows[rr, 0:512],
                                               -1.0) if VE else None)
                m_ab = M("V")
                W(TE, "V", m_ab)
                t and t.matmul(p45[:, 0:512], ones_rf[:, :], rows[0:1, 512:1024],
                               start=True, stop=True)
                t and t.matmul(p45[:, 512:1024], onesF[64:65, :], rows[64:65, 512:1024],
                               start=True, stop=True)
                t and t.matmul(p23[:, 0:512], ones_rf[:, :], rows[0:1, 0:512],
                               start=True, stop=True)
                _li = TE and t.matmul(p23[:, 512:1024], onesF[64:65, :], rows[64:65, 0:512],
                               start=True, stop=True)
                m_bc = TMILE(_li if TE else None)
                for icy in range(EB):
                    W(VE, "T", m_bc)
                    for st in range(2):
                        sl = slice(icy * SL + st * 512, icy * SL + (st + 1) * 512)
                        ph = slice(st * 512, (st + 1) * 512)
                        I("V", v.tensor_mul(sqs[:, ph], y_buf[:, sl], p45[:, ph]) if VE else None)
                        I("V", v.tensor_add(y_buf[:, sl], sqs[:, ph], p23[:, ph]) if VE else None)
                return M("V")

            m_y1 = ln_finalize(bufA)
            BARRIER()
            if phases <= 1:
                return

            epi_store = [0]
            xuser = [0]
            wsb_free = [0]

            def project(wdram, oc_list, rhs_buf, rhs_mark, use_gp_dma, epi):
                hmarks = {}
                accfree = {}
                for ix, oc in enumerate(oc_list):
                    DE = GP if use_gp_dma else SY
                    dname = "DG" if use_gp_dma else "DY"
                    de = g if use_gp_dma else y
                    accT = p01 if ix % 2 == 0 else p23
                    for hf in range(2):
                        hidx = ix * 2 + hf
                        spar = hidx % 2
                        if hidx >= 2:
                            W(DE, "T", hmarks[hidx - 2])
                        else:
                            W(DE, "T", wsb_free[0])
                        src = wdram[hf * 2048:(hf + 1) * 2048, oc * 128:(oc + 1) * 128]
                        src = src.rearrange("(ic p) o -> p ic o", p=128)
                        I(dname, de.dma_start(
                            out=wsb[:, spar * 2048:(spar + 1) * 2048]
                            .rearrange("p (ic o) -> p ic o", ic=16),
                            in_=src) if DE else None, 16)
                        m_w = M(dname)
                        if hidx == 0 and rhs_mark is not None:
                            W(TE, "V", rhs_mark)
                        W(TE, dname, m_w)
                        if hf == 0 and ix >= 2:
                            W(TE, *accfree[ix - 2])
                        _li = None
                        for st in range(2):
                            for icp in range(16):
                                icg = hf * 16 + icp
                                _li = TE and t.matmul(
                                    accT[:, st * 512:(st + 1) * 512],
                                    wsb[:, spar * 2048 + icp * 128:spar * 2048 + (icp + 1) * 128],
                                    rhs_buf[:, icg * SL + st * 512:icg * SL + (st + 1) * 512],
                                    start=(icg == 0), stop=(icg == EB - 1))
                        hmarks[hidx] = TMILE(_li if TE else None)
                    wsb_free[0] = hmarks[ix * 2 + 1]
                    accfree[ix] = epi(ix, oc, accT, hmarks[ix * 2 + 1])

            def ffn_epi_factory(xsrc, xdst, cast_buf, cast_marks):
                def epi(ix, oc, accT, m_mm):
                    W(SC, "T", m_mm)
                    W(SC, "V", xuser[0])
                    for st in range(2):
                        I("C", c.activation(tsc[:, st * 512:(st + 1) * 512],
                                            accT[:, st * 512:(st + 1) * 512], AF.Identity,
                                            bias=hbP[:, oc:oc + 1], scale=0.5) if SC else None)
                    m_t = M("C")
                    W(SY, "V", xuser[0])
                    I("DY", y.dma_start(out=xin[:, :],
                                        in_=xsrc[oc * 128:(oc + 1) * 128, :]) if SY else None, 16)
                    m_xi = M("DY")
                    W(VE, "C", m_t)
                    W(VE, "DY", m_xi)
                    W(VE, "DY", epi_store[0])
                    I("V", v.tensor_add(xout[:, :], xin[:, :], tsc[:, :]) if VE else None)
                    xuser[0] = M("V")
                    if cast_buf is not None:
                        I("V", v.tensor_copy(cast_buf[:, oc * SL:(oc + 1) * SL],
                                             xout[:, :]) if VE else None)
                        cast_marks.append(M("V"))
                    m_done = M("V")
                    W(SY, "V", m_done)
                    I("DY", y.dma_start(out=xdst[oc * 128:(oc + 1) * 128, :],
                                        in_=xout[:, :]) if SY else None, 16)
                    epi_store[0] = M("DY")
                    return ("C", m_t)
                return epi

            x2casts = []
            W(SY, "CC1", 1)
            project(w_full.ap(), list(range(EB)), bufA, m_y1, False,
                    ffn_epi_factory(xT, x2_d, bufB, x2casts))

            stat2 = []
            for icx in range(EB):
                if icx >= 1:
                    W(SC, "T", stat2[icx - 1])
                W(SC, "V", x2casts[icx])
                I("C", c.activation(sqs[:, :], bufB[:, icx * SL:(icx + 1) * SL],
                                    AF.Square) if SC else None)
                m_sq = M("C")
                if icx == 0:
                    W(TE, "V", M("V"))
                W(TE, "C", m_sq)
                first, last = icx == 0, icx == EB - 1
                t and t.matmul(p67[0:1, 0:512], ones_c[:, :],
                               bufB[:, icx * SL:icx * SL + 512], start=first, stop=last)
                t and t.matmul(p67[64:65, 0:512], ones_c[:, :],
                               bufB[:, icx * SL + 512:(icx + 1) * SL], start=first, stop=last)
                t and t.matmul(p67[0:1, 512:1024], ones_c[:, :],
                               sqs[:, 0:512], start=first, stop=last)
                _li = TE and t.matmul(p67[64:65, 512:1024], ones_c[:, :],
                               sqs[:, 512:1024], start=first, stop=last)
                stat2.append(TMILE(_li if TE else None))
            m_y2 = ln_finalize(bufB)
            BARRIER()
            if dump == "y2":
                dst = outT.rearrange("(ic p) n -> p ic n", p=128)
                src_ = bufB[:, :].rearrange("p (ic n) -> p ic n", ic=EB)
                I("DG", g.dma_start(out=dst, in_=src_) if GP else None, 16)
                W(SY, "DG")
                W(SY, "DY")
                return
            if dump == "x2":
                I("DY", y.dma_start(out=outT[:, :], in_=x2_d[:, :]) if SY else None, 16)
                W(SY, "DY")
                return
            if phases <= 2:
                return

            if dump == "kp":
                for hf in range(2):
                    src_ = wkT[hf * 2048:(hf + 1) * 2048, 0:128]
                    src_ = src_.rearrange("(ic p) o -> p ic o", p=128)
                    I("DG", g.dma_start(
                        out=wsb[:, hf * 2048:(hf + 1) * 2048]
                        .rearrange("p (ic o) -> p ic o", ic=16),
                        in_=src_) if GP else None, 16)
                m_w = M("DG")
                W(TE, "DG", m_w)
                W(TE, "V", m_y2)
                _li = None
                for hf in range(2):
                    for st in range(2):
                        for icp in range(16):
                            icg = hf * 16 + icp
                            _li = TE and t.matmul(
                                p01[:, st * 512:(st + 1) * 512],
                                wsb[:, hf * 2048 + icp * 128:hf * 2048 + (icp + 1) * 128],
                                bufB[:, icg * SL + st * 512:icg * SL + (st + 1) * 512],
                                start=(icg == 0), stop=(icg == EB - 1))
                m_mm0 = TMILE(_li if TE else None)
                W(VE, "T", m_mm0)
                I("V", v.tensor_copy(qtmp[:, 0:512], p01[:, 0:512]) if VE else None)
                I("V", v.tensor_copy(qtmp[:, 512:1024], p01[:, 512:1024]) if VE else None)
                m_cpd = M("V")
                W(GP, "V", m_cpd)
                I("DG", g.dma_start(out=outT[0:128, :], in_=qtmp[:, :]) if GP else None, 16)
                dstw = outT.rearrange("e n -> (e n)")[1024 * 1024:1024 * 1024 + 128 * 4096]                    .rearrange("(p m) -> p m", p=128)
                I("DG", g.dma_start(out=dstw, in_=wsb[:, :]) if GP else None, 16)
                W(SY, "DG")
                W(SY, "DY")
                return

            kv_store = [0]

            def k_epi(ix, oc, accT, m_mm):
                W(VE, "T", m_mm)
                W(VE, "DY", kv_store[0])
                I("V", v.tensor_copy(qtmp[:, 0:512], accT[:, 0:512]) if VE else None)
                I("V", v.tensor_copy(qtmp[:, 512:1024], accT[:, 512:1024]) if VE else None)
                m_cp = M("V")
                W(TE, "V", m_cp)
                _li = None
                for st in range(2):
                    _li = TE and t.matmul(p45[:, st * 512:(st + 1) * 512], rotm[:, :],
                                          qtmp[:, st * 512:(st + 1) * 512], start=True, stop=True)
                m_rot = TMILE(_li if TE else None)
                W(VE, "T", m_rot)
                for st in range(2):
                    I("V", v.tensor_mul(tsc[:, 0:512], qtmp[:, st * 512:(st + 1) * 512],
                                        cos_bf[:, st * 512:(st + 1) * 512]) if VE else None)
                    I("V", v.tensor_mul(tsc[:, 512:1024], p45[:, st * 512:(st + 1) * 512],
                                        sin_bf[:, st * 512:(st + 1) * 512]) if VE else None)
                    I("V", v.tensor_add(kstage[:, st * 512:(st + 1) * 512],
                                        tsc[:, 0:512], tsc[:, 512:1024]) if VE else None)
                m_k = M("V")
                W(SY, "V", m_k)
                I("DY", y.dma_start(out=k_loc.ap()[oc:oc + 1, :]
                                    .rearrange("a (p n) -> (a p) n", p=128),
                                    in_=kstage[:, :]) if SY else None, 16)
                kv_store[0] = M("DY")
                return ("V", m_cp)

            def v_epi(ix, oc, accT, m_mm):
                W(VE, "T", m_mm)
                W(VE, "DY", kv_store[0])
                for st in range(2):
                    I("V", v.tensor_copy(tsc[:, st * 512:(st + 1) * 512],
                                         accT[:, st * 512:(st + 1) * 512]) if VE else None)
                m_vs = M("V")
                W(TE, "V", m_vs)
                _li = None
                for ccx in range(8):
                    _li = TE and t.transpose(p45[:, ccx * 128:(ccx + 1) * 128],
                                             tsc[:, ccx * 128:(ccx + 1) * 128], identf[:, :])
                m_tr = TMILE(_li if TE else None)
                W(VE, "T", m_tr)
                I("V", v.tensor_copy(vtr_sb[:, 0:512], p45[:, 0:512]) if VE else None)
                I("V", v.tensor_copy(vtr_sb[:, 512:1024], p45[:, 512:1024]) if VE else None)
                m_vt = M("V")
                W(SY, "V", m_vt)
                I("DY", y.dma_start(out=v_loc.ap()[oc:oc + 1, :]
                                    .rearrange("a (p n) -> (a p) n", p=128),
                                    in_=vtr_sb[:, :]) if SY else None, 16)
                kv_store[0] = M("DY")
                return ("V", m_vt)

            project(wkT, list(range(KVH)), bufB, m_y2, True, k_epi)
            if dump == "kp4x":
                pass
            if dump == "kp3":
                W(GP, "DY", M("DY"))
                dst = outT.rearrange("e n -> (e n)")[0:KVH * D * SL].rearrange(
                    "(r m) -> r m", r=KVH)
                I("DG", g.dma_start(out=dst, in_=k_loc.ap()) if GP else None, 16)
                W(SY, "DG")
                W(SY, "DY")
                return
            if phases <= 2.5:
                return
            project(wvT, list(range(KVH)), bufB, None, True, v_epi)
            if dump == "kp4":
                W(GP, "DY", M("DY"))
                dst = outT.rearrange("e n -> (e n)")[0:KVH * D * SL].rearrange(
                    "(r m) -> r m", r=KVH)
                I("DG", g.dma_start(out=dst, in_=k_loc.ap()) if GP else None, 16)
                W(SY, "DG")
                W(SY, "DY")
                return
            W(GP, "DY", M("DY"))
            ICC("CC2", g.collective_compute(
                "AllGather", ALU.bypass, replica_groups=[list(range(8))],
                ins=[k_loc.ap()], outs=[k_full.ap()]) if GP else None)
            ICC("CC2", g.collective_compute(
                "AllGather", ALU.bypass, replica_groups=[list(range(8))],
                ins=[v_loc.ap()], outs=[v_full.ap()]) if GP else None)
            BARRIER()
            if phases <= 3:
                return

            kfl = k_full.ap().rearrange("r (p n) -> (r p) n", p=128)
            vfl = v_full.ap().rearrange("r (p n) -> (r p) n", p=128)
            att_norm = [0]
            att_mm = [0]
            ast_store = [0, 0]
            for kv in range(KVH):
                if kv > 0:
                    W(GP, "T", att_mm[0])
                for r in range(NCORES):
                    col = r * 8 + kv
                    I("DG", g.indirect_dma_start(
                        ksb[:, r * SL:(r + 1) * SL], None, kfl,
                        IndirectOffsetOnAxis(ap=idx_s[:, col:col + 1], axis=0)) if GP else None, 16)
                    I("DG", g.indirect_dma_start(
                        vsb[:, r * SL:(r + 1) * SL], None, vfl,
                        IndirectOffsetOnAxis(ap=idx_s[:, col:col + 1], axis=0)) if GP else None, 16)
                m_gather = M("DG")

                kn = []
                knv = [0]
                for blk in range(8):
                    W(SC, "DG", m_gather)
                    if blk >= 1:
                        W(SC, "T", kn[blk - 1])
                    I("C", c.activation(sqs[:, :], ksb[:, blk * SL:(blk + 1) * SL],
                                        AF.Square) if SC else None)
                    m_k2 = M("C")
                    W(TE, "C", m_k2)
                    if blk == 0:
                        W(TE, "V", att_norm[0])
                    else:
                        W(TE, "V", knv[0])
                    t and t.matmul(p45[0:1, 0:512], ones_c[:, :],
                                   sqs[:, 0:512], start=True, stop=True)
                    _li = TE and t.matmul(p45[0:1, 512:1024], ones_c[:, :],
                                          sqs[:, 512:1024], start=True, stop=True)
                    kn.append(TMILE(_li if TE else None))
                    W(VE, "T", kn[-1])
                    I("V", v.reduce_max(rows[0:1, 0:1], p45[0:1, 0:512], axis=AX.X) if VE else None)
                    I("V", v.reduce_max(rows[0:1, 1:2], p45[0:1, 512:1024], axis=AX.X) if VE else None)
                    I("V", v.tensor_max(rows[0:1, 0:1], rows[0:1, 0:1], rows[0:1, 1:2]) if VE else None)
                    if blk == 0:
                        I("V", v.tensor_copy(bnd[0:1, 0:1], rows[0:1, 0:1]) if VE else None)
                    else:
                        I("V", v.tensor_max(bnd[0:1, 0:1], bnd[0:1, 0:1],
                                            rows[0:1, 0:1]) if VE else None)
                    knv[0] = M("V")
                m_kmax = M("V")

                for pair in range(2):
                    qd0 = kv * 4 + pair * 2
                    for qi, qd in enumerate((qd0, qd0 + 1)):
                        for hf in range(2):
                            spar = hf
                            W(GP, "T", att_mm[0])
                            W(GP, "T", wsb_free[0])
                            W(GP, "C", M("C"))
                            src = wqT[hf * 2048:(hf + 1) * 2048, qd * 128:(qd + 1) * 128]
                            src = src.rearrange("(ic p) o -> p ic o", p=128)
                            I("DG", g.dma_start(
                                out=wsb[:, spar * 2048:(spar + 1) * 2048]
                                .rearrange("p (ic o) -> p ic o", ic=16),
                                in_=src) if GP else None, 16)
                            m_wq = M("DG")
                            W(TE, "DG", m_wq)
                            if kv == 0 and pair == 0 and qi == 0 and hf == 0:
                                W(TE, "V", m_y2)
                            if hf == 0:
                                W(TE, "C", M("C"))
                            accT = p01 if qi == 0 else p23
                            for st in range(2):
                                for icp in range(16):
                                    icg = hf * 16 + icp
                                    _li = TE and t.matmul(
                                        accT[:, st * 512:(st + 1) * 512],
                                        wsb[:, spar * 2048 + icp * 128:spar * 2048 + (icp + 1) * 128],
                                        bufB[:, icg * SL + st * 512:icg * SL + (st + 1) * 512],
                                        start=(icg == 0), stop=(icg == EB - 1))
                        m_qmm = TMILE(_li if TE else None)
                        wsb_free[0] = m_qmm
                        accT = p01 if qi == 0 else p23
                        W(VE, "T", m_qmm)
                        I("V", v.tensor_copy(qtmp[:, 0:512], accT[:, 0:512]) if VE else None)
                        I("V", v.tensor_copy(qtmp[:, 512:1024], accT[:, 512:1024]) if VE else None)
                        m_cp = M("V")
                        W(TE, "V", m_cp)
                        _li = None
                        for st in range(2):
                            _li = TE and t.matmul(p45[:, st * 512:(st + 1) * 512], rotm[:, :],
                                                  qtmp[:, st * 512:(st + 1) * 512],
                                                  start=True, stop=True)
                        m_rot = TMILE(_li if TE else None)
                        W(VE, "T", m_rot)
                        for st in range(2):
                            I("V", v.tensor_mul(tsc[:, 0:512], qtmp[:, st * 512:(st + 1) * 512],
                                                cos_bf[:, st * 512:(st + 1) * 512]) if VE else None)
                            I("V", v.tensor_mul(tsc[:, 512:1024],
                                                p45[:, st * 512:(st + 1) * 512],
                                                sin_bf[:, st * 512:(st + 1) * 512]) if VE else None)
                            I("V", v.tensor_add(qsb[:, qi * SL + st * 512:qi * SL + (st + 1) * 512],
                                                tsc[:, 0:512], tsc[:, 512:1024]) if VE else None)
                    m_qrope = M("V")
                    for qi in range(2):
                        W(SC, "V", m_qrope)
                        W(SC, "T", M("T"))
                        I("C", c.activation(sqs[:, :], qsb[:, qi * SL:(qi + 1) * SL],
                                            AF.Square) if SC else None)
                        m_q2 = M("C")
                        W(TE, "C", m_q2)
                        if qi == 0:
                            W(TE, "V", att_norm[0])
                            W(TE, "V", m_kmax)
                        else:
                            W(TE, "V", M("V"))
                        t and t.matmul(p45[0:1, 0:512], ones_c[:, :], sqs[:, 0:512],
                                       start=True, stop=True)
                        _li = TE and t.matmul(p45[0:1, 512:1024], ones_c[:, :], sqs[:, 512:1024],
                                              start=True, stop=True)
                        m_qn = TMILE(_li if TE else None)
                        W(VE, "T", m_qn)
                        I("V", v.reduce_max(rows[0:1, 0:1], p45[0:1, 0:512], axis=AX.X) if VE else None)
                        I("V", v.reduce_max(rows[0:1, 1:2], p45[0:1, 512:1024], axis=AX.X) if VE else None)
                        I("V", v.tensor_max(rows[0:1, 0:1], rows[0:1, 0:1],
                                            rows[0:1, 1:2]) if VE else None)
                        if qi == 0:
                            I("V", v.tensor_copy(bnd[0:1, 1:2], rows[0:1, 0:1]) if VE else None)
                        else:
                            I("V", v.tensor_max(bnd[0:1, 1:2], bnd[0:1, 1:2],
                                                rows[0:1, 0:1]) if VE else None)
                    I("V", v.tensor_mul(bnd[0:1, 2:3], bnd[0:1, 0:1], bnd[0:1, 1:2]) if VE else None)
                    m_b1 = M("V")
                    W(SC, "V", m_b1)
                    I("C", c.activation(bnd[0:1, 3:4], bnd[0:1, 2:3], AF.Ln) if SC else None)
                    I("C", c.activation(bnd[0:1, 3:4], bnd[0:1, 3:4], AF.Exp, scale=0.5) if SC else None)
                    I("C", c.activation(bnd[0:1, 4:5], bnd[0:1, 3:4], AF.Copy,
                                        scale=-SCL) if SC else None)
                    m_b2 = M("C")
                    W(VE, "C", m_b2)
                    I("V", v.tensor_scalar_add(bnd[0:1, 4:5], bnd[0:1, 4:5],
                                               -0.5) if VE else None)
                    m_b2v = M("V")
                    W(TE, "V", m_b2v)
                    _li = TE and t.matmul(p45[:, 512:513], ones_rf[:, :], bnd[0:1, 4:5],
                                          start=True, stop=True)
                    m_b3 = TMILE(_li if TE else None)
                    W(VE, "T", m_b3)
                    I("V", v.tensor_copy(bcol[:, :], p45[:, 512:513]) if VE else None)
                    m_bcol = M("V")

                    for j in range(2):
                        sched = SCHED[j]
                        n = len(sched)
                        expm = []
                        scm = []
                        attm = []
                        for nx, (ci, mk) in enumerate(sched):
                            par = nx % 2
                            if nx >= 2:
                                W(TE, "C", expm[nx - 2])
                            if nx == 0:
                                W(TE, "V", m_bcol)
                                W(TE, "V", att_norm[0])
                                W(TE, "DG", m_gather)
                            _li = None
                            for qi in range(2):
                                accS = (p01 if qi == 0 else p23)[:, par * 512:(par + 1) * 512]
                                _li = TE and t.matmul(accS, ksb[:, ci * 128:(ci + 1) * 128],
                                                      qsb[:, qi * SL + j * 512:qi * SL + (j + 1) * 512],
                                                      start=True, stop=True)
                            scm.append(TMILE(_li if TE else None))
                            W(SC, "T", scm[nx])
                            if nx >= 2:
                                W(SC, "T", attm[nx - 2])
                            for qi in range(2):
                                accS = (p01 if qi == 0 else p23)[:, par * 512:(par + 1) * 512]
                                pslot = ptile[:, (qi * 2 + par) * 512:(qi * 2 + par + 1) * 512]
                                ins = c.activation(pslot, accS, AF.Exp, bias=bcol[:, :],
                                                   scale=SCL) if SC else None
                                if qi == 1:
                                    I("C", ins)
                            expm.append(M("C"))
                            if mk is not None:
                                if mk == 'W':
                                    moff = 8 * 512
                                elif mk == 'N':
                                    moff = 9 * 512
                                else:
                                    kind, m = mk
                                    moff = (m if kind == 'L' else 4 + m) * 512
                                W(VE, "C", expm[nx])
                                for qi in range(2):
                                    pslot = ptile[:, (qi * 2 + par) * 512:(qi * 2 + par + 1) * 512]
                                    ins = v.tensor_mul(pslot, pslot,
                                                       masks[:, moff:moff + 512]) if VE else None
                                    if qi == 1:
                                        I("V", ins)
                                W(TE, "V", M("V"))
                            else:
                                W(TE, "C", expm[nx])
                            first, last = nx == 0, nx == n - 1
                            _li = None
                            for qi in range(2):
                                pslot = ptile[:, (qi * 2 + par) * 512:(qi * 2 + par + 1) * 512]
                                t and t.matmul(p67[:, qi * 512:(qi + 1) * 512],
                                               vsb[:, ci * 128:(ci + 1) * 128], pslot,
                                               start=first, stop=last)
                                dslot = p45[0:1, 0:512] if qi == 0 else p45[64:65, 0:512]
                                _li = TE and t.matmul(dslot, ones_c[:, :], pslot,
                                                      start=first, stop=last)
                            attm.append(TMILE(_li if TE else None))
                        att_mm[0] = M("T")
                        W(VE, "T", att_mm[0])
                        I("V", v.tensor_copy(rows[0:1, 0:512], p45[0:1, 0:512]) if VE else None)
                        I("V", v.tensor_copy(rows[64:65, 0:512], p45[64:65, 0:512]) if VE else None)
                        I("V", v.reciprocal(rows[0:1, 512:1024], rows[0:1, 0:512]) if VE else None)
                        I("V", v.reciprocal(rows[64:65, 512:1024], rows[64:65, 0:512]) if VE else None)
                        m_inv = M("V")
                        W(TE, "V", m_inv)
                        t and t.matmul(p45[:, 0:512], ones_rf[:, :], rows[0:1, 512:1024],
                                       start=True, stop=True)
                        _li = TE and t.matmul(p45[:, 512:1024], onesF[64:65, :], rows[64:65, 512:1024],
                                              start=True, stop=True)
                        m_ibc = TMILE(_li if TE else None)
                        for qi in range(2):
                            qh = kv * 4 + pair * 2 + qi
                            aslot = ast[:, (qi) * SL + j * 512:(qi) * SL + (j + 1) * 512]
                            W(VE, "T", m_ibc)
                            W(VE, "DY", ast_store[qi])
                            I("V", v.tensor_copy(bcs[:, :], p45[:, qi * 512:(qi + 1) * 512]) if VE else None)
                            I("V", v.tensor_mul(aslot, p67[:, qi * 512:(qi + 1) * 512],
                                                bcs[:, :]) if VE else None)
                            m_n = M("V")
                            W(SY, "V", m_n)
                            I("DY", y.dma_start(out=att_d[qh, :, j * 512:(j + 1) * 512],
                                                in_=aslot) if SY else None, 16)
                            ast_store[qi] = M("DY")
                        att_norm[0] = M("V")
            BARRIER()
            if dump == "att":
                for hh in range(H):
                    I("DY", y.dma_start(out=outT[hh * 128:(hh + 1) * 128, :],
                                        in_=att_d[hh, :, :]) if SY else None, 16)
                W(SY, "DY")
                return
            if phases <= 4:
                return

            ring_w = [0, 0, 0, 0]
            ring_a = [0, 0, 0, 0]
            x3casts = {}
            m_epi2 = [0]
            for st in range(2):
                for ocb in range(4):
                    accs = [p01[:, 0:512], p01[:, 512:1024], p23[:, 0:512], p23[:, 512:1024],
                            p45[:, 0:512], p45[:, 512:1024], p67[:, 0:512], p67[:, 512:1024]]
                    for hd in range(H):
                        par = hd % 4
                        W(GP, "T", ring_w[par] if ring_w[par] > 0 else wsb_free[0])
                        src = woT[hd * 128:(hd + 1) * 128, ocb * 1024:(ocb + 1) * 1024]
                        I("DG", g.dma_start(out=wsb[:, par * 1024:(par + 1) * 1024],
                                            in_=src) if GP else None, 16)
                        m_wo = M("DG")
                        if ring_a[par] > 0:
                            W(SC, "T", ring_a[par])
                        I("DC", c.dma_start(out=ast[:, par * 512:(par + 1) * 512],
                                            in_=att_d[hd, :, st * 512:(st + 1) * 512]) if SC else None, 16)
                        m_at = M("DC")
                        W(TE, "DG", m_wo)
                        W(TE, "DC", m_at)
                        if hd == 0:
                            W(TE, "V", m_epi2[0] if (st, ocb) != (0, 0) else M("V"))
                        _li = None
                        for ocx in range(8):
                            _li = TE and t.matmul(accs[ocx],
                                                  wsb[:, par * 1024 + ocx * 128:par * 1024 + (ocx + 1) * 128],
                                                  ast[:, par * 512:(par + 1) * 512],
                                                  start=(hd == 0), stop=(hd == H - 1))
                        mt = TMILE(_li if TE else None)
                        ring_w[par] = mt
                        ring_a[par] = mt
                        wsb_free[0] = mt
                    m_wod = M("T")
                    for ocx in range(8):
                        oc = ocb * 8 + ocx
                        W(SY, "V", xuser[0])
                        I("DY", y.dma_start(out=xin[:, 0:512],
                                            in_=x2_d[oc * 128:(oc + 1) * 128,
                                                     st * 512:(st + 1) * 512]) if SY else None, 16)
                        m_xi = M("DY")
                        W(VE, "T", m_wod)
                        W(VE, "DY", m_xi)
                        W(VE, "DY", epi_store[0])
                        I("V", v.tensor_add(xout[:, 0:512], xin[:, 0:512], accs[ocx]) if VE else None)
                        xuser[0] = M("V")
                        I("V", v.tensor_copy(bufA[:, oc * SL + st * 512:oc * SL + (st + 1) * 512],
                                             xout[:, 0:512]) if VE else None)
                        x3casts[(oc, st)] = M("V")
                        W(SY, "V", x3casts[(oc, st)])
                        I("DY", y.dma_start(out=x3_d[oc * 128:(oc + 1) * 128,
                                                     st * 512:(st + 1) * 512],
                                            in_=xout[:, 0:512]) if SY else None, 16)
                        epi_store[0] = M("DY")
                    m_epi2[0] = M("V")
            BARRIER()
            if dump == "x3":
                I("DY", y.dma_start(out=outT[:, :], in_=x3_d[:, :]) if SY else None, 16)
                W(SY, "DY")
                return
            if phases <= 5:
                return

            stat3 = []
            for icx in range(EB):
                if icx >= 1:
                    W(SC, "T", stat3[icx - 1])
                I("C", c.activation(sqs[:, :], bufA[:, icx * SL:(icx + 1) * SL],
                                    AF.Square) if SC else None)
                m_sq = M("C")
                W(TE, "C", m_sq)
                first, last = icx == 0, icx == EB - 1
                t and t.matmul(p67[0:1, 0:512], ones_c[:, :],
                               bufA[:, icx * SL:icx * SL + 512], start=first, stop=last)
                t and t.matmul(p67[64:65, 0:512], ones_c[:, :],
                               bufA[:, icx * SL + 512:(icx + 1) * SL], start=first, stop=last)
                t and t.matmul(p67[0:1, 512:1024], ones_c[:, :],
                               sqs[:, 0:512], start=first, stop=last)
                _li = TE and t.matmul(p67[64:65, 512:1024], ones_c[:, :],
                               sqs[:, 512:1024], start=first, stop=last)
                stat3.append(TMILE(_li if TE else None))
            m_y3 = ln_finalize(bufA)
            BARRIER()
            if phases <= 6:
                return

            project(w_full.ap(), list(range(EB)), bufA, m_y3, False,
                    ffn_epi_factory(x3_d, outT, None, None))

            W(SY, "DY")

        @block.gpsimd
        def _(e):
            emit("g", e)

        @block.tensor
        def _(e):
            emit("t", e)

        @block.vector
        def _(e):
            emit("v", e)

        @block.scalar
        def _(e):
            emit("c", e)

        @block.sync
        def _(e):
            emit("y", e)

    return nc


def kernel(**inputs):
    f32 = np.float32
    x = np.asarray(inputs["L0_attn_hidden_states"], f32)[0]
    Wq = np.asarray(inputs["L0_attn_Wq"], f32)
    Wk = np.asarray(inputs["L0_attn_Wk"], f32)
    Wv = np.asarray(inputs["L0_attn_Wv"], f32)
    Wo = np.asarray(inputs["L0_attn_Wo"], f32)
    U_top = np.asarray(inputs["L0_ffn_U_top"], f32)
    S_top = np.asarray(inputs["L0_ffn_S_top"], f32)
    Vh_top = np.asarray(inputs["L0_ffn_Vh_top"], f32)
    U_tail = np.asarray(inputs["L0_ffn_U_tail"], f32)
    S_tail = np.asarray(inputs["L0_ffn_S_tail"], f32)
    Vh_tail = np.asarray(inputs["L0_ffn_Vh_tail"], f32)
    U_tail_r = np.asarray(inputs["L0_ffn_U_tail_r"], f32)
    S_tail_r = np.asarray(inputs["L0_ffn_S_tail_r"], f32)
    Vh_tail_r = np.asarray(inputs["L0_ffn_Vh_tail_r"], f32)
    alpha = np.asarray(inputs["L0_ffn_alpha"], f32)
    beta = np.asarray(inputs["L0_ffn_beta"], f32)
    Dv = np.asarray(inputs["L0_ffn_D"], f32)
    R = np.asarray(inputs["L0_ffn_R"], f32)
    bias = np.asarray(inputs["L0_ffn_bias"], f32)

    C = np.ascontiguousarray
    xTf = C(x.T)
    wqTh = C(Wq.T)
    wkTh = C(Wk.T)
    wvTh = C(Wv.T)
    woTh = C(Wo.T)
    utTh = np.zeros((E, E), f32)
    utTh[:U_tail.shape[1], :] = U_tail.T
    st_pad = np.zeros((E,), f32)
    st_pad[:S_tail.shape[0]] = S_tail
    stPh = C(st_pad.reshape(RCH, 128).T)
    biasPh = C(bias.reshape(EB, 128).T)
    utopTh = C(U_top.T)
    utrTh = C(U_tail_r.T)

    in_maps = []
    for cc in range(NCORES):
        vh = np.zeros((E, ISL), f32)
        vh[:Vh_tail.shape[0], :] = Vh_tail[:, cc * ISL:(cc + 1) * ISL]
        idx = np.zeros((128, 64), np.int32)
        for r in range(NCORES):
            rot = (cc + r) % NCORES
            for h in range(KVH):
                idx[:, r * 8 + h] = (rot * 8 + h) * 128 + np.arange(128)
        in_maps.append({
            "xT": C(xTf[:, cc * SL:(cc + 1) * SL]),
            "wqT": wqTh, "wkT": wkTh, "wvT": wvTh, "woT": woTh,
            "utT": utTh, "vhsl": vh, "stP": stPh, "biasP": biasPh,
            "utopT": utopTh, "vhtop": C(Vh_top[:, cc * ISL:(cc + 1) * ISL]),
            "utrT": utrTh, "vhtr": C(Vh_tail_r[:, cc * ISL:(cc + 1) * ISL]),
            "r4": R, "stop_r": S_top.reshape(1, 64), "alpha_r": alpha.reshape(1, 64),
            "beta_r": beta.reshape(1, 64), "str_r": S_tail_r.reshape(1, 4),
            "dv_r": Dv.reshape(1, 4),
            "pos0": np.array([[cc * SL]], f32),
            "kvidx": idx,
            "eflags": np.concatenate([
                np.full((128, 1), 1.0 if cc == NCORES - 1 else 0.0, f32),
                np.full((128, 1), 1.0 if cc == 0 else 0.0, f32)], axis=1),
        })

    nc = build_nc()
    res = run_bass_kernel_spmd(nc, in_maps, core_ids=list(range(NCORES)))
    outs = [np.asarray(res.results[ccc]["outT"]) for ccc in range(NCORES)]
    full = np.concatenate(outs, axis=1)
    return C(full.T).reshape(1, S, E).astype(np.float32)


# version 04
# speedup vs baseline: 1.4491x; 1.4491x over previous
import numpy as np
from contextlib import ExitStack

import concourse.bass as bass
import concourse.mybir as mybir
from concourse.bass import IndirectOffsetOnAxis
from concourse.bass_utils import run_bass_kernel_spmd

F32 = mybir.dt.float32
BF16 = mybir.dt.bfloat16
I32 = mybir.dt.int32
AF = mybir.ActivationFunctionType
ALU = mybir.AluOpType
AX = mybir.AxisListType

NCORES = 8
S = 8192
SL = 1024
E = 4096
EB = 32
D = 128
H = 32
KVH = 8
ISL = 512
OT = 8
RCH = 32
SCL = 0.08838834764831845
LN10K = 9.210340371976184
INV2PI = 0.15915494309189535
TWOPI = 6.283185307179586
PIH = 1.5707963267948966
MAGIC = 12582912.0
EPS = 1e-5


def _sched(j):
    out = []
    if j == 0:
        for i in range(8, 12):
            out.append((i, ('L', i - 8)))
        for i in range(12, 56):
            out.append((i, None))
        for i in range(56, 60):
            out.append((i, ('R', i - 56)))
        for i in range(60, 64):
            out.append((i, 'N'))
    else:
        for i in range(8, 12):
            out.append((i, 'W'))
        for i in range(12, 16):
            out.append((i, ('L', i - 12)))
        for i in range(16, 60):
            out.append((i, None))
        for i in range(60, 64):
            out.append((i, ('R', i - 60)))
    return out


SCHED = [_sched(0), _sched(1)]


def build_nc(phases=99.0, dump=None):
    nc = bass.Bass("TRN2", target_bir_lowering=False, debug=False)

    dI = lambda n, s, dt: nc.dram_tensor(n, s, dt, kind="ExternalInput").ap()
    xT = dI("xT", [E, SL], F32)
    wqT = dI("wqT", [E, E], F32)
    wkT = dI("wkT", [E, KVH * D], F32)
    wvT = dI("wvT", [E, KVH * D], F32)
    woT = dI("woT", [E, E], F32)
    utT = dI("utT", [E, E], F32)
    vhsl = dI("vhsl", [E, ISL], F32)
    stP = dI("stP", [128, RCH], F32)
    biasP = dI("biasP", [128, EB], F32)
    utopT = dI("utopT", [64, E], F32)
    vhtop = dI("vhtop", [64, ISL], F32)
    utrT = dI("utrT", [4, E], F32)
    vhtr = dI("vhtr", [4, ISL], F32)
    r4 = dI("r4", [4, 4], F32)
    stop_r = dI("stop_r", [1, 64], F32)
    alpha_r = dI("alpha_r", [1, 64], F32)
    beta_r = dI("beta_r", [1, 64], F32)
    str_r = dI("str_r", [1, 4], F32)
    dv_r = dI("dv_r", [1, 4], F32)
    pos0 = dI("pos0", [1, 1], F32)
    kvidx = dI("kvidx", [128, 64], I32)
    eflags = dI("eflags", [128, 2], F32)

    outT = nc.dram_tensor("outT", [E, SL], F32, kind="ExternalOutput").ap()

    w_loc = nc.dram_tensor("w_loc", [ISL, E], BF16, kind="Internal")
    w_full = nc.dram_tensor("w_full", [E, E], BF16, kind="Internal")
    k_loc = nc.dram_tensor("k_loc", [KVH, D * SL], BF16, kind="Internal")
    k_full = nc.dram_tensor("k_full", [NCORES * KVH, D * SL], BF16, kind="Internal")
    v_loc = nc.dram_tensor("v_loc", [KVH, D * SL], BF16, kind="Internal")
    v_full = nc.dram_tensor("v_full", [NCORES * KVH, D * SL], BF16, kind="Internal")
    att_d = nc.dram_tensor("att_d", [H, 128, SL], BF16, kind="Internal").ap()
    kn_loc = nc.dram_tensor("kn_loc", [1, 8], F32, kind="Internal")
    kn_out = nc.dram_tensor("kn_out", [1, 8], F32, kind="Internal")
    x2_d = nc.dram_tensor("x2_d", [E, SL], F32, kind="Internal").ap()
    x3_d = nc.dram_tensor("x3_d", [E, SL], F32, kind="Internal").ap()

    with ExitStack() as ctx:
        sb = lambda n, s, dt: ctx.enter_context(nc.sbuf_tensor(n, s, dt))
        bufB = sb("bufB", [128, EB * SL], BF16)
        wsb = sb("wsb", [128, 4096], BF16)
        xin = sb("xin", [128, SL], F32)
        xout = sb("xout", [128, SL], F32)
        tsc = sb("tsc", [128, SL], F32)
        sqs = sb("sqs", [128, SL], BF16)
        rows = sb("rows", [128, SL], F32)
        bcs = sb("bcs", [128, 512], F32)
        cos_bf = sb("cos_bf", [128, SL], BF16)
        sin_bf = sb("sin_bf", [128, SL], BF16)
        masks = sb("masks", [128, 10 * 512], BF16)
        ast = sb("ast", [128, 2 * SL], BF16)
        ones_c = sb("ones_c", [128, 1], BF16)
        ones_rf = sb("ones_rf", [1, 128], F32)
        onesF = sb("onesF", [128, 128], F32)
        identf = sb("identf", [128, 128], F32)
        idx_s = sb("idx_s", [128, 64], I32)
        efl_s = sb("efl_s", [128, 2], F32)
        pos_s = sb("pos_s", [1, 1], F32)
        freqc = sb("freqc", [128, 1], F32)
        biasP_s = sb("biasP_s", [128, EB], F32)
        hbP = sb("hbP", [128, EB], F32)
        stP_s = sb("stP_s", [128, RCH], F32)
        vhtop_sb = sb("vhtop_sb", [64, ISL], BF16)
        vhtr_sb = sb("vhtr_sb", [4, ISL], BF16)
        r4_sb = sb("r4_sb", [4, 4], BF16)
        srow = sb("srow", [1, 64], F32)
        arow = sb("arow", [1, 64], F32)
        brow = sb("brow", [1, 64], F32)
        sh_r = sb("sh_r", [1, 64], F32)
        shc = sb("shc", [64, 1], F32)
        str_s = sb("str_s", [1, 4], F32)
        dv_s = sb("dv_s", [1, 4], F32)
        sd4r = sb("sd4r", [1, 4], F32)
        sd4c = sb("sd4c", [4, 1], F32)
        bnd = sb("bnd", [1, 8], F32)
        kng = sb("kng", [1, 8], F32)
        bcol = sb("bcol", [128, 1], F32)
        rotm = sb("rotm", [128, 128], BF16)
        qtmp = sb("qtmp", [128, SL], BF16)
        c64 = sb("c64", [128, 1], F32)
        g_sb = sb("g_sb", [4, E], BF16)
        frm = ExitStack()
        vh_sb = frm.enter_context(nc.sbuf_tensor("vh_sb", [128, RCH * ISL], BF16))
        ust = frm.enter_context(nc.sbuf_tensor("ust", [128, 2 * 512], BF16))
        wev = frm.enter_context(nc.sbuf_tensor("wev", [128, 2 * 512], BF16))
        utop_sb = frm.enter_context(nc.sbuf_tensor("utop_sb", [64, E], BF16))
        utr_sb = frm.enter_context(nc.sbuf_tensor("utr_sb", [4, E], BF16))
        tmp1 = frm.enter_context(nc.sbuf_tensor("tmp1", [128, SL], F32))
        trow = frm.enter_context(nc.sbuf_tensor("trow", [1, SL], F32))
        mtmp = frm.enter_context(nc.sbuf_tensor("mtmp", [128, 512], BF16))
        frm.close()
        att = ExitStack()
        ksb = att.enter_context(nc.sbuf_tensor("ksb", [128, S], BF16))
        vsb = att.enter_context(nc.sbuf_tensor("vsb", [128, S], BF16))
        qsb = att.enter_context(nc.sbuf_tensor("qsb", [128, 2 * SL], BF16))
        ptile = att.enter_context(nc.sbuf_tensor("ptile", [128, 8 * 512], BF16))
        att.close()
        bufA = sb("bufA", [128, EB * SL], BF16)
        kstage = qsb[:, 0:SL]
        vtr_sb = qsb[:, SL:2 * SL]

        p01 = ctx.enter_context(nc.psum_tensor("p01", [128, 1024], F32))
        p23 = ctx.enter_context(nc.psum_tensor("p23", [128, 1024], F32))
        p45 = ctx.enter_context(nc.psum_tensor("p45", [128, 1024], F32))
        p67 = ctx.enter_context(nc.psum_tensor("p67", [128, 1024], F32))

        sems = {}
        for name in ["T", "V", "C", "G", "DG", "DY", "DC", "CC1", "CC2"]:
            sems[name] = ctx.enter_context(nc.semaphore("s" + name))

        block = ctx.enter_context(nc.Block())

        def emit(which, eng):
            cnt = {k: 0 for k in sems}
            TE = which == "t"
            VE = which == "v"
            SC = which == "c"
            GP = which == "g"
            SY = which == "y"
            t = eng if TE else None
            v = eng if VE else None
            c = eng if SC else None
            g = eng if GP else None
            y = eng if SY else None

            def I(name, ins, k=1):
                cnt[name] += k
                if ins is not None:
                    ins.then_inc(sems[name], k)

            def ICC(name, ins):
                cnt[name] += 1
                if ins is not None:
                    ins.then_inc(sems[name])

            def W(active, name, val=None):
                if active:
                    v_ = cnt[name] if val is None else val
                    if v_ > 0:
                        eng.wait_ge(sems[name], v_)

            def TMILE(ins=None):
                cnt["T"] += 1
                if TE:
                    if ins is not None:
                        ins.then_inc(sems["T"], 1)
                    else:
                        eng.nop().then_inc(sems["T"], 1)
                return cnt["T"]

            def BARRIER():
                for n in sems:
                    W(True, n)

            M = lambda name: cnt[name]

            for (dst, src) in [
                (pos_s[:, :], pos0), (idx_s[:, :], kvidx),
                (biasP_s[:, :], biasP), (stP_s[:, :], stP),
                (utop_sb[:, :], utopT), (vhtop_sb[:, :], vhtop),
                (utr_sb[:, :], utrT), (vhtr_sb[:, :], vhtr),
                (r4_sb[:, :], r4), (efl_s[:, :], eflags),
            ]:
                I("DG", g.dma_start(out=dst, in_=src[:, :]) if GP else None, 16)
            for (dst, src) in [(srow, stop_r), (arow, alpha_r), (brow, beta_r),
                               (str_s, str_r), (dv_s, dv_r)]:
                I("DG", g.dma_start(out=dst[:, :], in_=src[:, :]) if GP else None, 16)
            m_smalls = M("DG")
            I("G", g.memset(ones_c[:, :], 1.0) if GP else None)
            I("G", g.memset(ones_rf[:, :], 1.0) if GP else None)
            I("G", g.memset(onesF[:, :], 1.0) if GP else None)
            I("G", g.memset(mtmp[:, :], 1.0) if GP else None)
            I("G", g.memset(tsc[:, 0:128], 1.0) if GP else None)
            W(GP, "G")
            I("G", g.affine_select(identf[:, :], tsc[:, 0:128], [[1, 128]],
                                   ALU.is_equal, 0.0, channel_multiplier=-1) if GP else None)
            for m in range(4):
                I("G", g.affine_select(masks[:, m * 512:(m + 1) * 512], mtmp[:, :],
                                       [[-1, 512]], ALU.is_ge, 0.0, base=128 * m,
                                       channel_multiplier=1) if GP else None)
            for m in range(4):
                I("G", g.affine_select(masks[:, (4 + m) * 512:(5 + m) * 512], mtmp[:, :],
                                       [[1, 512]], ALU.is_ge, 0.0, base=-128 * m,
                                       channel_multiplier=-1) if GP else None)
            I("G", g.memset(masks[:, 8 * 512:10 * 512], 0.0) if GP else None)
            I("G", g.iota(trow[:, :], [[1, SL]], base=0, channel_multiplier=0,
                          allow_small_or_imprecise_dtypes=True) if GP else None)
            I("G", g.iota(freqc[:, :], [[1, 1]], base=0, channel_multiplier=1,
                          allow_small_or_imprecise_dtypes=True) if GP else None)
            I("G", g.memset(c64[:, :], 64.0) if GP else None)
            W(GP, "G")
            I("G", g.affine_select(bcol[:, :], c64[:, :], [[1, 1]], ALU.is_ge, 0.0,
                                   base=-64, channel_multiplier=1) if GP else None)
            I("G", g.affine_select(sqs[:, 0:128], mtmp[:, 0:128], [[-1, 128]],
                                   ALU.is_equal, 0.0, base=64,
                                   channel_multiplier=1) if GP else None)
            I("G", g.affine_select(sqs[:, 128:256], mtmp[:, 0:128], [[-1, 128]],
                                   ALU.is_equal, 0.0, base=-64,
                                   channel_multiplier=1) if GP else None)
            m_ginit = M("G")
            W(VE, "G", m_ginit)
            I("V", v.tensor_sub(rotm[:, :], sqs[:, 0:128], sqs[:, 128:256]) if VE else None)
            if phases <= 0.05:
                return

            if dump == "gen":
                W(GP, "G", m_ginit)
                I("DG", g.dma_start(out=outT[0:1, 0:1024], in_=trow[:, :]) if GP else None, 16)
                I("DG", g.dma_start(out=outT[128:256, 0:1], in_=freqc[:, :], allow_slow_non_contiguous=True) if GP else None, 16)
                I("DG", g.dma_start(out=outT[256:384, 0:1], in_=bcol[:, :], allow_slow_non_contiguous=True) if GP else None, 16)
                I("DG", g.dma_start(out=outT[384:512, 0:512], in_=masks[:, 0:512]) if GP else None, 16)
                I("DG", g.dma_start(out=outT[512:640, 0:128], in_=identf[:, :]) if GP else None, 16)
                I("DG", g.dma_start(out=outT[640:768, 0:256], in_=sqs[:, 0:256]) if GP else None, 16)
                W(SY, "DG")
                W(SY, "DY")
                return

            W(VE, "G", m_ginit)
            W(VE, "DG", m_smalls)
            I("V", v.tensor_scalar_add(trow[:, :], trow[:, :], pos_s[0:1, 0:1]) if VE else None)
            m_trow = M("V")
            if phases <= 0.055:
                return
            W(VE, "G", m_ginit)
            I("V", v.tensor_sub(freqc[:, :], freqc[:, :], bcol[:, :]) if VE else None)
            m_fq1 = M("V")
            W(SC, "V", m_fq1)
            I("C", c.activation(freqc[:, :], freqc[:, :], AF.Exp,
                                scale=-LN10K / 64.0) if SC else None)
            m_freq0 = M("C")
            W(VE, "C", m_freq0)
            m_freq = M("V")
            if phases <= 0.06:
                return
            W(TE, "V", m_trow)
            W(TE, "G", m_ginit)
            t and t.matmul(p01[:, 0:512], ones_rf[:, :], trow[0:1, 0:512], start=True, stop=True)
            _li = TE and t.matmul(p01[:, 512:1024], ones_rf[:, :], trow[0:1, 512:1024],
                           start=True, stop=True)
            m_tbc = TMILE(_li if TE else None)
            if phases <= 0.08:
                return
            W(VE, "T", m_tbc)
            W(VE, "V", m_freq)
            I("V", v.tensor_copy(tmp1[:, 0:512], p01[:, 0:512]) if VE else None)
            I("V", v.tensor_copy(tmp1[:, 512:1024], p01[:, 512:1024]) if VE else None)
            if phases <= 0.09:
                return
            I("V", v.tensor_scalar_mul(xin[:, :], tmp1[:, :], freqc[:, :]) if VE else None)
            if phases <= 0.1:
                return

            def sin_table(dst_bf, _unused=None):
                I("V", v.tensor_scalar_mul(xout[:, :], xin[:, :], INV2PI) if VE else None)
                I("V", v.tensor_scalar_add(tmp1[:, :], xout[:, :], MAGIC) if VE else None)
                I("V", v.tensor_scalar_add(tmp1[:, :], tmp1[:, :], -MAGIC) if VE else None)
                I("V", v.tensor_sub(xout[:, :], xout[:, :], tmp1[:, :]) if VE else None)
                I("V", v.tensor_scalar_mul(xout[:, :], xout[:, :], TWOPI) if VE else None)
                mv = M("V")
                W(SC, "V", mv)
                I("C", c.activation(tmp1[:, :], xout[:, :], AF.Sin) if SC else None)
                mc = M("C")
                W(VE, "C", mc)
                I("V", v.tensor_copy(dst_bf[:, :], tmp1[:, :]) if VE else None)
                return M("V")

            sin_table(sin_bf, False)
            I("V", v.tensor_scalar_add(xin[:, :], xin[:, :], PIH) if VE else None)
            m_tabs = sin_table(cos_bf)
            if phases <= 0.12:
                return

            I("V", v.tensor_mul(sh_r[:, :], srow[:, :], arow[:, :]) if VE else None)
            I("V", v.tensor_add(sh_r[:, :], sh_r[:, :], brow[:, :]) if VE else None)
            I("V", v.tensor_relu(sh_r[:, :], sh_r[:, :]) if VE else None)
            I("V", v.tensor_relu(dv_s[:, :], dv_s[:, :]) if VE else None)
            I("V", v.tensor_mul(sd4r[:, :], str_s[:, :], dv_s[:, :]) if VE else None)
            I("V", v.tensor_scalar_mul(hbP[:, :], biasP_s[:, :], 0.5) if VE else None)
            for sl8 in range(4):
                I("V", v.tensor_scalar_max(masks[:, sl8 * 512:(sl8 + 1) * 512],
                                           masks[:, sl8 * 512:(sl8 + 1) * 512],
                                           efl_s[:, 0:1]) if VE else None)
            for sl8 in range(4):
                I("V", v.tensor_scalar_max(masks[:, (4 + sl8) * 512:(5 + sl8) * 512],
                                           masks[:, (4 + sl8) * 512:(5 + sl8) * 512],
                                           efl_s[:, 1:2]) if VE else None)
            I("V", v.tensor_scalar_max(masks[:, 8 * 512:9 * 512],
                                       masks[:, 8 * 512:9 * 512], efl_s[:, 0:1]) if VE else None)
            I("V", v.tensor_scalar_max(masks[:, 9 * 512:10 * 512],
                                       masks[:, 9 * 512:10 * 512], efl_s[:, 1:2]) if VE else None)
            m_smallprep = M("V")
            if phases <= 0.15:
                return
            W(TE, "V", m_smallprep)
            t and t.matmul(p45[0:64, 0:1], sh_r[:, :], ones_rf[0:1, 0:1], start=True, stop=True)
            _li = TE and t.matmul(p45[0:4, 1:2], sd4r[:, :], ones_rf[0:1, 0:1], start=True, stop=True)
            m_cols = TMILE(_li if TE else None)
            W(VE, "T", m_cols)
            I("V", v.tensor_copy(shc[:, :], p45[0:64, 0:1]) if VE else None)
            I("V", v.tensor_copy(sd4c[:, :], p45[0:4, 1:2]) if VE else None)
            I("V", v.tensor_scalar(utop_sb[:, :], utop_sb[:, :], shc[:, :], None,
                                   ALU.mult) if VE else None)
            I("V", v.tensor_scalar(utr_sb[:, :], utr_sb[:, :], sd4c[:, :], None,
                                   ALU.mult) if VE else None)
            m_usc = M("V")
            if phases <= 0.18:
                return
            for otl in range(8):
                if otl == 0:
                    W(TE, "V", m_usc)
                else:
                    W(TE, "V")
                _li = TE and t.matmul(p45[0:4, 0:512], r4_sb[:, :],
                               utr_sb[:, otl * 512:(otl + 1) * 512], start=True, stop=True)
                mt = TMILE(_li if TE else None)
                W(VE, "T", mt)
                I("V", v.tensor_copy(g_sb[:, otl * 512:(otl + 1) * 512],
                                     p45[0:4, 0:512]) if VE else None)
            m_g = M("V")
            if phases <= 0.2:
                return

            if dump == "tab0":
                W(GP, "V", m_tabs)
                I("DG", g.dma_start(out=outT[0:128, :], in_=cos_bf[:, :]) if GP else None, 16)
                I("DG", g.dma_start(out=outT[128:256, :], in_=sin_bf[:, :]) if GP else None, 16)
                I("DG", g.dma_start(out=outT[256:384, 0:1], in_=freqc[:, :], allow_slow_non_contiguous=True) if GP else None, 16)
                I("DG", g.dma_start(out=outT[384:512, 0:128], in_=rotm[:, :]) if GP else None, 16)
                W(SY, "DG")
                W(SY, "DY")
                return

            vh_view = vhsl.rearrange("(rc p) i -> p rc i", p=128)
            I("DG", g.dma_start(out=vh_sb[:, :].rearrange("p (rc i) -> p rc i", rc=RCH),
                                in_=vh_view) if GP else None, 16)
            m_vh = M("DG")
            W(VE, "DG", m_vh)
            for rc in range(RCH):
                I("V", v.tensor_scalar(vh_sb[:, rc * 512:(rc + 1) * 512],
                                       vh_sb[:, rc * 512:(rc + 1) * 512],
                                       stP_s[:, rc:rc + 1], None, ALU.mult) if VE else None)
            m_vhs = M("V")

            frm_marks = {}
            wev_marks = [0, 0]
            prev_evac = [0]
            for ot in range(OT):
                for rc in range(RCH):
                    par = (ot * RCH + rc) % 2
                    prev = ot * RCH + rc - 2
                    if prev >= 0:
                        W(GP, "T", frm_marks[prev])
                    src = utT[rc * 128:(rc + 1) * 128, ot * 512:(ot + 1) * 512]
                    I("DG", g.dma_start(out=ust[:, par * 512:(par + 1) * 512],
                                        in_=src) if GP else None, 16)
                    m_u = M("DG")
                    if ot == 0 and rc == 0:
                        W(TE, "V", m_vhs)
                        W(TE, "V", m_g)
                    if rc == 0 and ot > 0:
                        W(TE, "V", prev_evac[0])
                    W(TE, "DG", m_u)
                    _li = None
                    for ic in range(4):
                        acc = (p01 if ic < 2 else p23)[:, (ic % 2) * 512:(ic % 2 + 1) * 512]
                        _li = TE and t.matmul(acc,
                                              vh_sb[:, rc * 512 + ic * 128:rc * 512 + (ic + 1) * 128],
                                              ust[:, par * 512:(par + 1) * 512],
                                              start=(rc == 0), stop=False)
                    frm_marks[ot * RCH + rc] = TMILE(_li if TE else None)
                for ic in range(4):
                    wpar = ic % 2
                    acc = (p01 if ic < 2 else p23)[:, (ic % 2) * 512:(ic % 2 + 1) * 512]
                    t and t.matmul(acc, vhtop_sb[:, ic * 128:(ic + 1) * 128],
                                   utop_sb[:, ot * 512:(ot + 1) * 512], start=False, stop=False)
                    W(TE, "DY", wev_marks[wpar])
                    _li = TE and t.matmul(acc, vhtr_sb[:, ic * 128:(ic + 1) * 128],
                                   g_sb[:, ot * 512:(ot + 1) * 512], start=False, stop=True)
                    mt = TMILE(_li if TE else None)
                    W(VE, "T", mt)
                    W(VE, "DY", wev_marks[wpar])
                    I("V", v.tensor_copy(wev[:, wpar * 512:(wpar + 1) * 512], acc) if VE else None)
                    mv = M("V")
                    W(SY, "V", mv)
                    I("DY", y.dma_start(out=w_loc.ap()[ic * 128:(ic + 1) * 128,
                                                       ot * 512:(ot + 1) * 512],
                                        in_=wev[:, wpar * 512:(wpar + 1) * 512]) if SY else None, 16)
                    wev_marks[wpar] = M("DY")
                prev_evac[0] = M("V")
            m_wloc = M("DY")
            if phases <= 0.5:
                return
            W(GP, "DY", m_wloc)
            ICC("CC1", g.collective_compute(
                "AllGather", ALU.bypass, replica_groups=[list(range(8))],
                ins=[w_loc.ap()], outs=[w_full.ap()]) if GP else None)
            if phases <= 0.7:
                return

            if dump == "tab":
                W(GP, "V", m_tabs)
                I("DG", g.dma_start(out=outT[0:128, :], in_=cos_bf[:, :]) if GP else None, 16)
                I("DG", g.dma_start(out=outT[128:256, :], in_=sin_bf[:, :]) if GP else None, 16)
                I("DG", g.dma_start(out=outT[256:384, 0:1], in_=freqc[:, :], allow_slow_non_contiguous=True) if GP else None, 16)
                I("DG", g.dma_start(out=outT[384:512, 0:128], in_=rotm[:, :]) if GP else None, 16)
                W(SY, "DG")
                W(SY, "DY")
                return

            castm = []
            stat_marks = []
            for icx in range(EB):
                W(SY, "V", m_tabs)
                W(SY, "DY", m_wloc)
                if icx > 0:
                    W(SY, "V", castm[icx - 1])
                I("DY", y.dma_start(out=xin[:, :],
                                    in_=xT[icx * 128:(icx + 1) * 128, :]) if SY else None, 16)
                m_x = M("DY")
                W(VE, "DY", m_x)
                if icx == 0:
                    W(VE, "DY", m_wloc)
                I("V", v.tensor_copy(bufA[:, icx * SL:(icx + 1) * SL], xin[:, :]) if VE else None)
                castm.append(M("V"))
                if icx >= 1:
                    W(SC, "T", stat_marks[icx - 1])
                W(SC, "V", castm[icx])
                I("C", c.activation(sqs[:, :], bufA[:, icx * SL:(icx + 1) * SL],
                                    AF.Square) if SC else None)
                m_sq = M("C")
                W(TE, "C", m_sq)
                first, last = icx == 0, icx == EB - 1
                t and t.matmul(p67[0:1, 0:512], ones_c[:, :],
                               bufA[:, icx * SL:icx * SL + 512], start=first, stop=last)
                t and t.matmul(p67[64:65, 0:512], ones_c[:, :],
                               bufA[:, icx * SL + 512:(icx + 1) * SL], start=first, stop=last)
                t and t.matmul(p67[0:1, 512:1024], ones_c[:, :],
                               sqs[:, 0:512], start=first, stop=last)
                _li = TE and t.matmul(p67[64:65, 512:1024], ones_c[:, :],
                               sqs[:, 512:1024], start=first, stop=last)
                stat_marks.append(TMILE(_li if TE else None))
            if phases <= 0.9:
                return

            def ln_finalize(y_buf):
                W(VE, "T", M("T"))
                for r in (0, 64):
                    rr = slice(r, r + 1)
                    I("V", v.tensor_scalar_mul(rows[rr, 0:512], p67[rr, 0:512],
                                               1.0 / E) if VE else None)
                    I("V", v.tensor_scalar_mul(rows[rr, 512:1024], p67[rr, 512:1024],
                                               1.0 / E) if VE else None)
                    I("V", v.tensor_mul(bcs[rr, 0:512], rows[rr, 0:512],
                                        rows[rr, 0:512]) if VE else None)
                    I("V", v.tensor_sub(rows[rr, 512:1024], rows[rr, 512:1024],
                                        bcs[rr, 0:512]) if VE else None)
                    I("V", v.tensor_scalar_add(rows[rr, 512:1024], rows[rr, 512:1024],
                                               EPS) if VE else None)
                m_var = M("V")
                W(SC, "V", m_var)
                for r in (0, 64):
                    rr = slice(r, r + 1)
                    I("C", c.activation(rows[rr, 512:1024], rows[rr, 512:1024],
                                        AF.Ln) if SC else None)
                    I("C", c.activation(rows[rr, 512:1024], rows[rr, 512:1024], AF.Exp,
                                        scale=-0.5) if SC else None)
                m_rstd = M("C")
                W(VE, "C", m_rstd)
                for r in (0, 64):
                    rr = slice(r, r + 1)
                    I("V", v.tensor_mul(rows[rr, 0:512], rows[rr, 0:512],
                                        rows[rr, 512:1024]) if VE else None)
                    I("V", v.tensor_scalar_mul(rows[rr, 0:512], rows[rr, 0:512],
                                               -1.0) if VE else None)
                m_ab = M("V")
                W(TE, "V", m_ab)
                t and t.matmul(p45[:, 0:512], ones_rf[:, :], rows[0:1, 512:1024],
                               start=True, stop=True)
                t and t.matmul(p45[:, 512:1024], onesF[64:65, :], rows[64:65, 512:1024],
                               start=True, stop=True)
                t and t.matmul(p23[:, 0:512], ones_rf[:, :], rows[0:1, 0:512],
                               start=True, stop=True)
                _li = TE and t.matmul(p23[:, 512:1024], onesF[64:65, :], rows[64:65, 0:512],
                               start=True, stop=True)
                m_bc = TMILE(_li if TE else None)
                for icy in range(EB):
                    W(VE, "T", m_bc)
                    for st in range(2):
                        sl = slice(icy * SL + st * 512, icy * SL + (st + 1) * 512)
                        ph = slice(st * 512, (st + 1) * 512)
                        I("V", v.tensor_mul(sqs[:, ph], y_buf[:, sl], p45[:, ph]) if VE else None)
                        I("V", v.tensor_add(y_buf[:, sl], sqs[:, ph], p23[:, ph]) if VE else None)
                return M("V")

            m_y1 = ln_finalize(bufA)
            BARRIER()
            if phases <= 1:
                return

            epi_store = [0]
            xuser = [0]
            wsb_free = [0]

            def project(wdram, oc_list, rhs_buf, rhs_mark, use_gp_dma, epi):
                hmarks = {}
                accfree = {}
                for ix, oc in enumerate(oc_list):
                    DE = GP if use_gp_dma else SY
                    dname = "DG" if use_gp_dma else "DY"
                    de = g if use_gp_dma else y
                    accT = p01 if ix % 2 == 0 else p23
                    for hf in range(2):
                        hidx = ix * 2 + hf
                        spar = hidx % 2
                        if hidx >= 2:
                            W(DE, "T", hmarks[hidx - 2])
                        else:
                            W(DE, "T", wsb_free[0])
                        src = wdram[hf * 2048:(hf + 1) * 2048, oc * 128:(oc + 1) * 128]
                        src = src.rearrange("(ic p) o -> p ic o", p=128)
                        I(dname, de.dma_start(
                            out=wsb[:, spar * 2048:(spar + 1) * 2048]
                            .rearrange("p (ic o) -> p ic o", ic=16),
                            in_=src) if DE else None, 16)
                        m_w = M(dname)
                        if hidx == 0 and rhs_mark is not None:
                            W(TE, "V", rhs_mark)
                        W(TE, dname, m_w)
                        if hf == 0 and ix >= 2:
                            W(TE, *accfree[ix - 2])
                        _li = None
                        for st in range(2):
                            for icp in range(16):
                                icg = hf * 16 + icp
                                _li = TE and t.matmul(
                                    accT[:, st * 512:(st + 1) * 512],
                                    wsb[:, spar * 2048 + icp * 128:spar * 2048 + (icp + 1) * 128],
                                    rhs_buf[:, icg * SL + st * 512:icg * SL + (st + 1) * 512],
                                    start=(icg == 0), stop=(icg == EB - 1))
                        hmarks[hidx] = TMILE(_li if TE else None)
                    wsb_free[0] = hmarks[ix * 2 + 1]
                    accfree[ix] = epi(ix, oc, accT, hmarks[ix * 2 + 1])

            def ffn_epi_factory(xsrc, xdst, cast_buf, cast_marks):
                def epi(ix, oc, accT, m_mm):
                    W(SC, "T", m_mm)
                    W(SC, "V", xuser[0])
                    for st in range(2):
                        I("C", c.activation(tsc[:, st * 512:(st + 1) * 512],
                                            accT[:, st * 512:(st + 1) * 512], AF.Identity,
                                            bias=hbP[:, oc:oc + 1], scale=0.5) if SC else None)
                    m_t = M("C")
                    W(SY, "V", xuser[0])
                    I("DY", y.dma_start(out=xin[:, :],
                                        in_=xsrc[oc * 128:(oc + 1) * 128, :]) if SY else None, 16)
                    m_xi = M("DY")
                    W(VE, "C", m_t)
                    W(VE, "DY", m_xi)
                    W(VE, "DY", epi_store[0])
                    I("V", v.tensor_add(xout[:, :], xin[:, :], tsc[:, :]) if VE else None)
                    xuser[0] = M("V")
                    if cast_buf is not None:
                        I("V", v.tensor_copy(cast_buf[:, oc * SL:(oc + 1) * SL],
                                             xout[:, :]) if VE else None)
                        cast_marks.append(M("V"))
                    m_done = M("V")
                    W(SY, "V", m_done)
                    I("DY", y.dma_start(out=xdst[oc * 128:(oc + 1) * 128, :],
                                        in_=xout[:, :]) if SY else None, 16)
                    epi_store[0] = M("DY")
                    return ("C", m_t)
                return epi

            x2casts = []
            W(SY, "CC1", 1)
            project(w_full.ap(), list(range(EB)), bufA, m_y1, False,
                    ffn_epi_factory(xT, x2_d, bufB, x2casts))

            stat2 = []
            for icx in range(EB):
                if icx >= 1:
                    W(SC, "T", stat2[icx - 1])
                W(SC, "V", x2casts[icx])
                I("C", c.activation(sqs[:, :], bufB[:, icx * SL:(icx + 1) * SL],
                                    AF.Square) if SC else None)
                m_sq = M("C")
                if icx == 0:
                    W(TE, "V", M("V"))
                W(TE, "C", m_sq)
                first, last = icx == 0, icx == EB - 1
                t and t.matmul(p67[0:1, 0:512], ones_c[:, :],
                               bufB[:, icx * SL:icx * SL + 512], start=first, stop=last)
                t and t.matmul(p67[64:65, 0:512], ones_c[:, :],
                               bufB[:, icx * SL + 512:(icx + 1) * SL], start=first, stop=last)
                t and t.matmul(p67[0:1, 512:1024], ones_c[:, :],
                               sqs[:, 0:512], start=first, stop=last)
                _li = TE and t.matmul(p67[64:65, 512:1024], ones_c[:, :],
                               sqs[:, 512:1024], start=first, stop=last)
                stat2.append(TMILE(_li if TE else None))
            m_y2 = ln_finalize(bufB)
            BARRIER()
            if dump == "y2":
                dst = outT.rearrange("(ic p) n -> p ic n", p=128)
                src_ = bufB[:, :].rearrange("p (ic n) -> p ic n", ic=EB)
                I("DG", g.dma_start(out=dst, in_=src_) if GP else None, 16)
                W(SY, "DG")
                W(SY, "DY")
                return
            if dump == "x2":
                I("DY", y.dma_start(out=outT[:, :], in_=x2_d[:, :]) if SY else None, 16)
                W(SY, "DY")
                return
            if phases <= 2:
                return

            if dump == "kp":
                for hf in range(2):
                    src_ = wkT[hf * 2048:(hf + 1) * 2048, 0:128]
                    src_ = src_.rearrange("(ic p) o -> p ic o", p=128)
                    I("DG", g.dma_start(
                        out=wsb[:, hf * 2048:(hf + 1) * 2048]
                        .rearrange("p (ic o) -> p ic o", ic=16),
                        in_=src_) if GP else None, 16)
                m_w = M("DG")
                W(TE, "DG", m_w)
                W(TE, "V", m_y2)
                _li = None
                for hf in range(2):
                    for st in range(2):
                        for icp in range(16):
                            icg = hf * 16 + icp
                            _li = TE and t.matmul(
                                p01[:, st * 512:(st + 1) * 512],
                                wsb[:, hf * 2048 + icp * 128:hf * 2048 + (icp + 1) * 128],
                                bufB[:, icg * SL + st * 512:icg * SL + (st + 1) * 512],
                                start=(icg == 0), stop=(icg == EB - 1))
                m_mm0 = TMILE(_li if TE else None)
                W(VE, "T", m_mm0)
                I("V", v.tensor_copy(qtmp[:, 0:512], p01[:, 0:512]) if VE else None)
                I("V", v.tensor_copy(qtmp[:, 512:1024], p01[:, 512:1024]) if VE else None)
                m_cpd = M("V")
                W(GP, "V", m_cpd)
                I("DG", g.dma_start(out=outT[0:128, :], in_=qtmp[:, :]) if GP else None, 16)
                dstw = outT.rearrange("e n -> (e n)")[1024 * 1024:1024 * 1024 + 128 * 4096]                    .rearrange("(p m) -> p m", p=128)
                I("DG", g.dma_start(out=dstw, in_=wsb[:, :]) if GP else None, 16)
                W(SY, "DG")
                W(SY, "DY")
                return

            kv_store = [0]
            knm = [0]

            def k_epi(ix, oc, accT, m_mm):
                W(VE, "T", m_mm)
                W(VE, "DY", kv_store[0])
                I("V", v.tensor_copy(qtmp[:, 0:512], accT[:, 0:512]) if VE else None)
                I("V", v.tensor_copy(qtmp[:, 512:1024], accT[:, 512:1024]) if VE else None)
                m_cp = M("V")
                W(TE, "V", m_cp)
                _li = None
                for st in range(2):
                    _li = TE and t.matmul(p45[:, st * 512:(st + 1) * 512], rotm[:, :],
                                          qtmp[:, st * 512:(st + 1) * 512], start=True, stop=True)
                m_rot = TMILE(_li if TE else None)
                W(VE, "T", m_rot)
                for st in range(2):
                    I("V", v.tensor_mul(tsc[:, 0:512], qtmp[:, st * 512:(st + 1) * 512],
                                        cos_bf[:, st * 512:(st + 1) * 512]) if VE else None)
                    I("V", v.tensor_mul(tsc[:, 512:1024], p45[:, st * 512:(st + 1) * 512],
                                        sin_bf[:, st * 512:(st + 1) * 512]) if VE else None)
                    I("V", v.tensor_add(kstage[:, st * 512:(st + 1) * 512],
                                        tsc[:, 0:512], tsc[:, 512:1024]) if VE else None)
                m_k = M("V")
                W(SY, "V", m_k)
                I("DY", y.dma_start(out=k_loc.ap()[oc:oc + 1, :]
                                    .rearrange("a (p n) -> (a p) n", p=128),
                                    in_=kstage[:, :]) if SY else None, 16)
                kv_store[0] = M("DY")
                return ("V", m_cp)

            def v_epi(ix, oc, accT, m_mm):
                W(VE, "T", m_mm)
                W(VE, "DY", kv_store[0])
                for st in range(2):
                    I("V", v.tensor_copy(tsc[:, st * 512:(st + 1) * 512],
                                         accT[:, st * 512:(st + 1) * 512]) if VE else None)
                m_vs = M("V")
                W(TE, "V", m_vs)
                _li = None
                for ccx in range(8):
                    _li = TE and t.transpose(p45[:, ccx * 128:(ccx + 1) * 128],
                                             tsc[:, ccx * 128:(ccx + 1) * 128], identf[:, :])
                m_tr = TMILE(_li if TE else None)
                W(VE, "T", m_tr)
                I("V", v.tensor_copy(vtr_sb[:, 0:512], p45[:, 0:512]) if VE else None)
                I("V", v.tensor_copy(vtr_sb[:, 512:1024], p45[:, 512:1024]) if VE else None)
                m_vt = M("V")
                W(SY, "V", m_vt)
                I("DY", y.dma_start(out=v_loc.ap()[oc:oc + 1, :]
                                    .rearrange("a (p n) -> (a p) n", p=128),
                                    in_=vtr_sb[:, :]) if SY else None, 16)
                kv_store[0] = M("DY")
                return ("V", m_vt)

            project(wkT, list(range(KVH)), bufB, m_y2, True, k_epi)
            if dump == "kp4x":
                pass
            if dump == "kp3":
                W(GP, "DY", M("DY"))
                dst = outT.rearrange("e n -> (e n)")[0:KVH * D * SL].rearrange(
                    "(r m) -> r m", r=KVH)
                I("DG", g.dma_start(out=dst, in_=k_loc.ap()) if GP else None, 16)
                W(SY, "DG")
                W(SY, "DY")
                return
            if phases <= 2.5:
                return
            project(wvT, list(range(KVH)), bufB, None, True, v_epi)
            if dump == "kp4":
                W(GP, "DY", M("DY"))
                dst = outT.rearrange("e n -> (e n)")[0:KVH * D * SL].rearrange(
                    "(r m) -> r m", r=KVH)
                I("DG", g.dma_start(out=dst, in_=k_loc.ap()) if GP else None, 16)
                W(SY, "DG")
                W(SY, "DY")
                return
            W(GP, "DY", M("DY"))
            ICC("CC2", g.collective_compute(
                "AllGather", ALU.bypass, replica_groups=[list(range(8))],
                ins=[k_loc.ap()], outs=[k_full.ap()]) if GP else None)
            ICC("CC2", g.collective_compute(
                "AllGather", ALU.bypass, replica_groups=[list(range(8))],
                ins=[v_loc.ap()], outs=[v_full.ap()]) if GP else None)
            BARRIER()
            if phases <= 3:
                return

            kfl = k_full.ap().rearrange("r (p n) -> (r p) n", p=128)
            vfl = v_full.ap().rearrange("r (p n) -> (r p) n", p=128)
            att_norm = [0]
            att_mm = [0]
            ast_store = [0, 0]
            for kv in range(KVH):
                if kv > 0:
                    W(GP, "T", att_mm[0])
                for r in range(NCORES):
                    col = r * 8 + kv
                    I("DG", g.indirect_dma_start(
                        ksb[:, r * SL:(r + 1) * SL], None, kfl,
                        IndirectOffsetOnAxis(ap=idx_s[:, col:col + 1], axis=0)) if GP else None, 16)
                    I("DG", g.indirect_dma_start(
                        vsb[:, r * SL:(r + 1) * SL], None, vfl,
                        IndirectOffsetOnAxis(ap=idx_s[:, col:col + 1], axis=0)) if GP else None, 16)
                m_gather = M("DG")

                kn = []
                knv = [0]
                for blk in range(8):
                    W(SC, "DG", m_gather)
                    if blk >= 1:
                        W(SC, "T", kn[blk - 1])
                    I("C", c.activation(sqs[:, :], ksb[:, blk * SL:(blk + 1) * SL],
                                        AF.Square) if SC else None)
                    m_k2 = M("C")
                    W(TE, "C", m_k2)
                    if blk == 0:
                        W(TE, "V", att_norm[0])
                    else:
                        W(TE, "V", knv[0])
                    t and t.matmul(p45[0:1, 0:512], ones_c[:, :],
                                   sqs[:, 0:512], start=True, stop=True)
                    _li = TE and t.matmul(p45[0:1, 512:1024], ones_c[:, :],
                                          sqs[:, 512:1024], start=True, stop=True)
                    kn.append(TMILE(_li if TE else None))
                    W(VE, "T", kn[-1])
                    I("V", v.reduce_max(rows[0:1, 0:1], p45[0:1, 0:512], axis=AX.X) if VE else None)
                    I("V", v.reduce_max(rows[0:1, 1:2], p45[0:1, 512:1024], axis=AX.X) if VE else None)
                    I("V", v.tensor_max(rows[0:1, 0:1], rows[0:1, 0:1], rows[0:1, 1:2]) if VE else None)
                    if blk == 0:
                        I("V", v.tensor_copy(bnd[0:1, 0:1], rows[0:1, 0:1]) if VE else None)
                    else:
                        I("V", v.tensor_max(bnd[0:1, 0:1], bnd[0:1, 0:1],
                                            rows[0:1, 0:1]) if VE else None)
                    knv[0] = M("V")
                m_kmax = M("V")

                for pair in range(2):
                    qd0 = kv * 4 + pair * 2
                    for qi, qd in enumerate((qd0, qd0 + 1)):
                        for hf in range(2):
                            spar = hf
                            W(GP, "T", wsb_free[0])
                            src = wqT[hf * 2048:(hf + 1) * 2048, qd * 128:(qd + 1) * 128]
                            src = src.rearrange("(ic p) o -> p ic o", p=128)
                            I("DG", g.dma_start(
                                out=wsb[:, spar * 2048:(spar + 1) * 2048]
                                .rearrange("p (ic o) -> p ic o", ic=16),
                                in_=src) if GP else None, 16)
                            m_wq = M("DG")
                            W(TE, "DG", m_wq)
                            if kv == 0 and pair == 0 and qi == 0 and hf == 0:
                                W(TE, "V", m_y2)
                            if hf == 0:
                                W(TE, "C", M("C"))
                            accT = p01 if qi == 0 else p23
                            for st in range(2):
                                for icp in range(16):
                                    icg = hf * 16 + icp
                                    _li = TE and t.matmul(
                                        accT[:, st * 512:(st + 1) * 512],
                                        wsb[:, spar * 2048 + icp * 128:spar * 2048 + (icp + 1) * 128],
                                        bufB[:, icg * SL + st * 512:icg * SL + (st + 1) * 512],
                                        start=(icg == 0), stop=(icg == EB - 1))
                        m_qmm = TMILE(_li if TE else None)
                        wsb_free[0] = m_qmm
                        accT = p01 if qi == 0 else p23
                        W(VE, "T", m_qmm)
                        I("V", v.tensor_copy(qtmp[:, 0:512], accT[:, 0:512]) if VE else None)
                        I("V", v.tensor_copy(qtmp[:, 512:1024], accT[:, 512:1024]) if VE else None)
                        m_cp = M("V")
                        W(TE, "V", m_cp)
                        _li = None
                        for st in range(2):
                            _li = TE and t.matmul(p45[:, st * 512:(st + 1) * 512], rotm[:, :],
                                                  qtmp[:, st * 512:(st + 1) * 512],
                                                  start=True, stop=True)
                        m_rot = TMILE(_li if TE else None)
                        W(VE, "T", m_rot)
                        for st in range(2):
                            I("V", v.tensor_mul(tsc[:, 0:512], qtmp[:, st * 512:(st + 1) * 512],
                                                cos_bf[:, st * 512:(st + 1) * 512]) if VE else None)
                            I("V", v.tensor_mul(tsc[:, 512:1024],
                                                p45[:, st * 512:(st + 1) * 512],
                                                sin_bf[:, st * 512:(st + 1) * 512]) if VE else None)
                            I("V", v.tensor_add(qsb[:, qi * SL + st * 512:qi * SL + (st + 1) * 512],
                                                tsc[:, 0:512], tsc[:, 512:1024]) if VE else None)
                    m_qrope = M("V")
                    for qi in range(2):
                        W(SC, "V", m_qrope)
                        W(SC, "T", M("T"))
                        I("C", c.activation(sqs[:, :], qsb[:, qi * SL:(qi + 1) * SL],
                                            AF.Square) if SC else None)
                        m_q2 = M("C")
                        W(TE, "C", m_q2)
                        if qi == 0:
                            W(TE, "V", att_norm[0])
                            W(TE, "V", m_kmax)
                        else:
                            W(TE, "V", M("V"))
                        t and t.matmul(p45[0:1, 0:512], ones_c[:, :], sqs[:, 0:512],
                                       start=True, stop=True)
                        _li = TE and t.matmul(p45[0:1, 512:1024], ones_c[:, :], sqs[:, 512:1024],
                                              start=True, stop=True)
                        m_qn = TMILE(_li if TE else None)
                        W(VE, "T", m_qn)
                        I("V", v.reduce_max(rows[0:1, 0:1], p45[0:1, 0:512], axis=AX.X) if VE else None)
                        I("V", v.reduce_max(rows[0:1, 1:2], p45[0:1, 512:1024], axis=AX.X) if VE else None)
                        I("V", v.tensor_max(rows[0:1, 0:1], rows[0:1, 0:1],
                                            rows[0:1, 1:2]) if VE else None)
                        if qi == 0:
                            I("V", v.tensor_copy(bnd[0:1, 1:2], rows[0:1, 0:1]) if VE else None)
                        else:
                            I("V", v.tensor_max(bnd[0:1, 1:2], bnd[0:1, 1:2],
                                                rows[0:1, 0:1]) if VE else None)
                    I("V", v.tensor_mul(bnd[0:1, 2:3], bnd[0:1, 0:1],
                                        bnd[0:1, 1:2]) if VE else None)
                    m_b1 = M("V")
                    W(SC, "V", m_b1)
                    I("C", c.activation(bnd[0:1, 3:4], bnd[0:1, 2:3], AF.Ln) if SC else None)
                    I("C", c.activation(bnd[0:1, 3:4], bnd[0:1, 3:4], AF.Exp, scale=0.5) if SC else None)
                    I("C", c.activation(bnd[0:1, 4:5], bnd[0:1, 3:4], AF.Copy,
                                        scale=-SCL) if SC else None)
                    m_b2 = M("C")
                    W(VE, "C", m_b2)
                    I("V", v.tensor_scalar_add(bnd[0:1, 4:5], bnd[0:1, 4:5],
                                               -0.5) if VE else None)
                    m_b2v = M("V")
                    W(TE, "V", m_b2v)
                    _li = TE and t.matmul(p45[:, 512:513], ones_rf[:, :], bnd[0:1, 4:5],
                                          start=True, stop=True)
                    m_b3 = TMILE(_li if TE else None)
                    W(VE, "T", m_b3)
                    I("V", v.tensor_copy(bcol[:, :], p45[:, 512:513]) if VE else None)
                    m_bcol = M("V")

                    for j in range(2):
                        sched = SCHED[j]
                        steps = len(sched) // 2
                        expm = {}
                        scm = {}
                        attm = {}
                        maskm = {}
                        pbase = lambda m, qi, half: ((m % 2) * 4 + qi * 2 + half) * 512
                        for m in range(steps + 1):
                            if m < steps:
                                pair2 = sched[2 * m:2 * m + 2]
                                if m >= 1:
                                    W(TE, "C", expm[m - 1])
                                if m == 0:
                                    W(TE, "V", m_bcol)
                                    W(TE, "V", att_norm[0])
                                    W(TE, "DG", m_gather)
                                _li = None
                                for half, (ci, mk) in enumerate(pair2):
                                    for qi in range(2):
                                        accS = (p01 if qi == 0 else p23)[:, half * 512:(half + 1) * 512]
                                        _li = TE and t.matmul(
                                            accS, ksb[:, ci * 128:(ci + 1) * 128],
                                            qsb[:, qi * SL + j * 512:qi * SL + (j + 1) * 512],
                                            start=True, stop=True)
                                scm[m] = TMILE(_li if TE else None)
                                W(SC, "T", scm[m])
                                if m >= 2:
                                    W(SC, "T", attm[m - 2])
                                for qi in range(2):
                                    accS = (p01 if qi == 0 else p23)[:, 0:1024]
                                    pslot = ptile[:, pbase(m, qi, 0):pbase(m, qi, 0) + 1024]
                                    ins = c.activation(pslot, accS, AF.Exp, bias=bcol[:, :],
                                                       scale=SCL) if SC else None
                                    if qi == 1:
                                        I("C", ins)
                                expm[m] = M("C")
                                nmask = 0
                                for half, (ci, mk) in enumerate(pair2):
                                    if mk is None:
                                        continue
                                    if mk == 'W':
                                        moff = 8 * 512
                                    elif mk == 'N':
                                        moff = 9 * 512
                                    else:
                                        kind, mm_ = mk
                                        moff = (mm_ if kind == 'L' else 4 + mm_) * 512
                                    if nmask == 0:
                                        W(VE, "C", expm[m])
                                    nmask += 1
                                    for qi in range(2):
                                        pslot = ptile[:, pbase(m, qi, half):pbase(m, qi, half) + 512]
                                        ins = v.tensor_mul(pslot, pslot,
                                                           masks[:, moff:moff + 512]) if VE else None
                                        if qi == 1:
                                            I("V", ins)
                                maskm[m] = M("V") if nmask else 0
                            if m >= 1:
                                mp = m - 1
                                W(TE, "C", expm[mp])
                                if maskm[mp]:
                                    W(TE, "V", maskm[mp])
                                _li = None
                                for half, (ci, mk) in enumerate(sched[2 * mp:2 * mp + 2]):
                                    nglob = 2 * mp + half
                                    first, last = nglob == 0, nglob == 2 * steps - 1
                                    for qi in range(2):
                                        pslot = ptile[:, pbase(mp, qi, half):pbase(mp, qi, half) + 512]
                                        t and t.matmul(p67[:, qi * 512:(qi + 1) * 512],
                                                       vsb[:, ci * 128:(ci + 1) * 128], pslot,
                                                       start=first, stop=last)
                                for half, (ci, mk) in enumerate(sched[2 * mp:2 * mp + 2]):
                                    nglob = 2 * mp + half
                                    first, last = nglob == 0, nglob == 2 * steps - 1
                                    for qi in range(2):
                                        pslot = ptile[:, pbase(mp, qi, half):pbase(mp, qi, half) + 512]
                                        dslot = p45[0:1, 0:512] if qi == 0 else p45[64:65, 0:512]
                                        _li = TE and t.matmul(dslot, ones_c[:, :], pslot,
                                                              start=first, stop=last)
                                attm[mp] = TMILE(_li if TE else None)
                        att_mm[0] = M("T")
                        W(VE, "T", att_mm[0])
                        I("V", v.tensor_copy(rows[0:1, 0:512], p45[0:1, 0:512]) if VE else None)
                        I("V", v.tensor_copy(rows[64:65, 0:512], p45[64:65, 0:512]) if VE else None)
                        I("V", v.reciprocal(rows[0:1, 512:1024], rows[0:1, 0:512]) if VE else None)
                        I("V", v.reciprocal(rows[64:65, 512:1024], rows[64:65, 0:512]) if VE else None)
                        m_inv = M("V")
                        W(TE, "V", m_inv)
                        t and t.matmul(p45[:, 0:512], ones_rf[:, :], rows[0:1, 512:1024],
                                       start=True, stop=True)
                        _li = TE and t.matmul(p45[:, 512:1024], onesF[64:65, :], rows[64:65, 512:1024],
                                              start=True, stop=True)
                        m_ibc = TMILE(_li if TE else None)
                        for qi in range(2):
                            qh = kv * 4 + pair * 2 + qi
                            aslot = ast[:, (qi) * SL + j * 512:(qi) * SL + (j + 1) * 512]
                            W(VE, "T", m_ibc)
                            W(VE, "DY", ast_store[qi])
                            I("V", v.tensor_copy(bcs[:, :], p45[:, qi * 512:(qi + 1) * 512]) if VE else None)
                            I("V", v.tensor_mul(aslot, p67[:, qi * 512:(qi + 1) * 512],
                                                bcs[:, :]) if VE else None)
                            m_n = M("V")
                            W(SY, "V", m_n)
                            I("DY", y.dma_start(out=att_d[qh, :, j * 512:(j + 1) * 512],
                                                in_=aslot) if SY else None, 16)
                            ast_store[qi] = M("DY")
                        att_norm[0] = M("V")
            BARRIER()
            if dump == "att":
                for hh in range(H):
                    I("DY", y.dma_start(out=outT[hh * 128:(hh + 1) * 128, :],
                                        in_=att_d[hh, :, :]) if SY else None, 16)
                W(SY, "DY")
                return
            if phases <= 4:
                return

            ring_w = [0, 0, 0, 0]
            ring_a = [0, 0, 0, 0]
            x3casts = {}
            m_epi2 = [0]
            for st in range(2):
                for ocb in range(4):
                    accs = [p01[:, 0:512], p01[:, 512:1024], p23[:, 0:512], p23[:, 512:1024],
                            p45[:, 0:512], p45[:, 512:1024], p67[:, 0:512], p67[:, 512:1024]]
                    for hd in range(H):
                        par = hd % 4
                        W(GP, "T", ring_w[par] if ring_w[par] > 0 else wsb_free[0])
                        src = woT[hd * 128:(hd + 1) * 128, ocb * 1024:(ocb + 1) * 1024]
                        I("DG", g.dma_start(out=wsb[:, par * 1024:(par + 1) * 1024],
                                            in_=src) if GP else None, 16)
                        m_wo = M("DG")
                        if ring_a[par] > 0:
                            W(SC, "T", ring_a[par])
                        I("DC", c.dma_start(out=ast[:, par * 512:(par + 1) * 512],
                                            in_=att_d[hd, :, st * 512:(st + 1) * 512]) if SC else None, 16)
                        m_at = M("DC")
                        W(TE, "DG", m_wo)
                        W(TE, "DC", m_at)
                        if hd == 0:
                            W(TE, "V", m_epi2[0] if (st, ocb) != (0, 0) else M("V"))
                        _li = None
                        for ocx in range(8):
                            _li = TE and t.matmul(accs[ocx],
                                                  wsb[:, par * 1024 + ocx * 128:par * 1024 + (ocx + 1) * 128],
                                                  ast[:, par * 512:(par + 1) * 512],
                                                  start=(hd == 0), stop=(hd == H - 1))
                        mt = TMILE(_li if TE else None)
                        ring_w[par] = mt
                        ring_a[par] = mt
                        wsb_free[0] = mt
                    m_wod = M("T")
                    for ocx in range(8):
                        oc = ocb * 8 + ocx
                        W(SY, "V", xuser[0])
                        I("DY", y.dma_start(out=xin[:, 0:512],
                                            in_=x2_d[oc * 128:(oc + 1) * 128,
                                                     st * 512:(st + 1) * 512]) if SY else None, 16)
                        m_xi = M("DY")
                        W(VE, "T", m_wod)
                        W(VE, "DY", m_xi)
                        W(VE, "DY", epi_store[0])
                        I("V", v.tensor_add(xout[:, 0:512], xin[:, 0:512], accs[ocx]) if VE else None)
                        xuser[0] = M("V")
                        I("V", v.tensor_copy(bufA[:, oc * SL + st * 512:oc * SL + (st + 1) * 512],
                                             xout[:, 0:512]) if VE else None)
                        x3casts[(oc, st)] = M("V")
                        W(SY, "V", x3casts[(oc, st)])
                        I("DY", y.dma_start(out=x3_d[oc * 128:(oc + 1) * 128,
                                                     st * 512:(st + 1) * 512],
                                            in_=xout[:, 0:512]) if SY else None, 16)
                        epi_store[0] = M("DY")
                    m_epi2[0] = M("V")
            BARRIER()
            if dump == "x3":
                I("DY", y.dma_start(out=outT[:, :], in_=x3_d[:, :]) if SY else None, 16)
                W(SY, "DY")
                return
            if phases <= 5:
                return

            stat3 = []
            for icx in range(EB):
                if icx >= 1:
                    W(SC, "T", stat3[icx - 1])
                I("C", c.activation(sqs[:, :], bufA[:, icx * SL:(icx + 1) * SL],
                                    AF.Square) if SC else None)
                m_sq = M("C")
                W(TE, "C", m_sq)
                first, last = icx == 0, icx == EB - 1
                t and t.matmul(p67[0:1, 0:512], ones_c[:, :],
                               bufA[:, icx * SL:icx * SL + 512], start=first, stop=last)
                t and t.matmul(p67[64:65, 0:512], ones_c[:, :],
                               bufA[:, icx * SL + 512:(icx + 1) * SL], start=first, stop=last)
                t and t.matmul(p67[0:1, 512:1024], ones_c[:, :],
                               sqs[:, 0:512], start=first, stop=last)
                _li = TE and t.matmul(p67[64:65, 512:1024], ones_c[:, :],
                               sqs[:, 512:1024], start=first, stop=last)
                stat3.append(TMILE(_li if TE else None))
            m_y3 = ln_finalize(bufA)
            BARRIER()
            if phases <= 6:
                return

            project(w_full.ap(), list(range(EB)), bufA, m_y3, False,
                    ffn_epi_factory(x3_d, outT, None, None))

            W(SY, "DY")

        @block.gpsimd
        def _(e):
            emit("g", e)

        @block.tensor
        def _(e):
            emit("t", e)

        @block.vector
        def _(e):
            emit("v", e)

        @block.scalar
        def _(e):
            emit("c", e)

        @block.sync
        def _(e):
            emit("y", e)

    return nc


def kernel(**inputs):
    f32 = np.float32
    x = np.asarray(inputs["L0_attn_hidden_states"], f32)[0]
    Wq = np.asarray(inputs["L0_attn_Wq"], f32)
    Wk = np.asarray(inputs["L0_attn_Wk"], f32)
    Wv = np.asarray(inputs["L0_attn_Wv"], f32)
    Wo = np.asarray(inputs["L0_attn_Wo"], f32)
    U_top = np.asarray(inputs["L0_ffn_U_top"], f32)
    S_top = np.asarray(inputs["L0_ffn_S_top"], f32)
    Vh_top = np.asarray(inputs["L0_ffn_Vh_top"], f32)
    U_tail = np.asarray(inputs["L0_ffn_U_tail"], f32)
    S_tail = np.asarray(inputs["L0_ffn_S_tail"], f32)
    Vh_tail = np.asarray(inputs["L0_ffn_Vh_tail"], f32)
    U_tail_r = np.asarray(inputs["L0_ffn_U_tail_r"], f32)
    S_tail_r = np.asarray(inputs["L0_ffn_S_tail_r"], f32)
    Vh_tail_r = np.asarray(inputs["L0_ffn_Vh_tail_r"], f32)
    alpha = np.asarray(inputs["L0_ffn_alpha"], f32)
    beta = np.asarray(inputs["L0_ffn_beta"], f32)
    Dv = np.asarray(inputs["L0_ffn_D"], f32)
    R = np.asarray(inputs["L0_ffn_R"], f32)
    bias = np.asarray(inputs["L0_ffn_bias"], f32)

    C = np.ascontiguousarray
    xTf = C(x.T)
    wqTh = C(Wq.T)
    wkTh = C(Wk.T)
    wvTh = C(Wv.T)
    woTh = C(Wo.T)
    utTh = np.zeros((E, E), f32)
    utTh[:U_tail.shape[1], :] = U_tail.T
    st_pad = np.zeros((E,), f32)
    st_pad[:S_tail.shape[0]] = S_tail
    stPh = C(st_pad.reshape(RCH, 128).T)
    biasPh = C(bias.reshape(EB, 128).T)
    utopTh = C(U_top.T)
    utrTh = C(U_tail_r.T)

    in_maps = []
    for cc in range(NCORES):
        vh = np.zeros((E, ISL), f32)
        vh[:Vh_tail.shape[0], :] = Vh_tail[:, cc * ISL:(cc + 1) * ISL]
        idx = np.zeros((128, 64), np.int32)
        for r in range(NCORES):
            rot = (cc + r) % NCORES
            for h in range(KVH):
                idx[:, r * 8 + h] = (rot * 8 + h) * 128 + np.arange(128)
        in_maps.append({
            "xT": C(xTf[:, cc * SL:(cc + 1) * SL]),
            "wqT": wqTh, "wkT": wkTh, "wvT": wvTh, "woT": woTh,
            "utT": utTh, "vhsl": vh, "stP": stPh, "biasP": biasPh,
            "utopT": utopTh, "vhtop": C(Vh_top[:, cc * ISL:(cc + 1) * ISL]),
            "utrT": utrTh, "vhtr": C(Vh_tail_r[:, cc * ISL:(cc + 1) * ISL]),
            "r4": R, "stop_r": S_top.reshape(1, 64), "alpha_r": alpha.reshape(1, 64),
            "beta_r": beta.reshape(1, 64), "str_r": S_tail_r.reshape(1, 4),
            "dv_r": Dv.reshape(1, 4),
            "pos0": np.array([[cc * SL]], f32),
            "kvidx": idx,
            "eflags": np.concatenate([
                np.full((128, 1), 1.0 if cc == NCORES - 1 else 0.0, f32),
                np.full((128, 1), 1.0 if cc == 0 else 0.0, f32)], axis=1),
        })

    nc = build_nc()
    res = run_bass_kernel_spmd(nc, in_maps, core_ids=list(range(NCORES)))
    outs = [np.asarray(res.results[ccc]["outT"]) for ccc in range(NCORES)]
    full = np.concatenate(outs, axis=1)
    return C(full.T).reshape(1, S, E).astype(np.float32)
